# version 04
# baseline (speedup 1.0000x reference)
import contextlib
import os as _os
import numpy as np
import ml_dtypes
import concourse.bass as bass
import concourse.mybir as mybir
from concourse.bass_utils import run_bass_kernel_spmd

F32 = mybir.dt.float32
BF16 = mybir.dt.bfloat16
AF = mybir.ActivationFunctionType
ALU = mybir.AluOpType
AX = mybir.AxisListType.X
BF = ml_dtypes.bfloat16

D = 1024
T = 4096
NCH = 8
NG = 8
TG = 512
NEG = -1.0e30


def _build_constants():
    c = {}
    c["ones_bf"] = np.ones((128, 128), BF)
    c["ident_bf"] = np.eye(128, dtype=np.float32).astype(BF)
    c["ident_f32"] = np.eye(128, dtype=np.float32)
    ind2 = np.zeros((128, 2), np.float32)
    ind2[0:64, 0] = 1.0
    ind2[64:128, 1] = 1.0
    c["ind2_bf"] = ind2.astype(BF)
    ind = np.zeros((128, 128), np.float32)
    ind[0:64, 0:64] = 1.0
    ind[64:128, 64:128] = 1.0
    c["indrep_bf"] = ind.astype(BF)
    c["indrep8_bf"] = (ind * 0.125).astype(BF)
    pi = np.zeros((128, 128), np.float32)
    for blk in (0, 64):
        for i in range(32):
            pi[blk + 2 * i + 1, blk + 2 * i] = -1.0
            pi[blk + 2 * i, blk + 2 * i + 1] = 1.0
    c["rot_il_bf"] = pi.astype(BF)
    ph = np.zeros((128, 128), np.float32)
    for blk in (0, 64):
        for l in range(32):
            ph[blk + l + 32, blk + l] = -1.0
            ph[blk + l, blk + l + 32] = 1.0
    c["rot_hs_bf"] = ph.astype(BF)
    dh = 64
    t = np.arange(T, dtype=np.float64)
    inv = 10000.0 ** (-np.arange(0, dh, 2, dtype=np.float64) / dh)
    ang = np.outer(inv, t)
    cos2 = np.zeros((64, T)); sin2 = np.zeros((64, T))
    cos2[0::2] = np.cos(ang); cos2[1::2] = np.cos(ang)
    sin2[0::2] = np.sin(ang); sin2[1::2] = np.sin(ang)
    c["ropeA_cos"] = np.concatenate([cos2, cos2], 0).astype(np.float32)
    c["ropeA_sin"] = np.concatenate([sin2, sin2], 0).astype(np.float32)
    WINDOW = 512
    half_idx = np.arange(0, dh, 2, dtype=np.float64)
    inv_freq = 1.0 / 10000.0 ** (half_idx / dh)
    freqs = np.outer(inv_freq, t)
    base = (half_idx + 0.4 * dh) / (1.4 * dh)
    power = (t - T // 2) / float(WINDOW // 2)
    scale = base[:, None] ** power[None, :]
    cf, sf = np.cos(freqs), np.sin(freqs)
    qc = np.concatenate([cf * scale, cf * scale], 0) * 8.0
    qs = np.concatenate([sf * scale, sf * scale], 0) * 8.0
    kc = np.concatenate([cf / scale, cf / scale], 0)
    ks = np.concatenate([sf / scale, sf / scale], 0)
    c["xpos_qc"] = np.concatenate([qc, qc], 0).astype(np.float32)
    c["xpos_qs"] = np.concatenate([qs, qs], 0).astype(np.float32)
    c["xpos_kc"] = np.concatenate([kc, kc], 0).astype(np.float32)
    c["xpos_ks"] = np.concatenate([ks, ks], 0).astype(np.float32)
    i = np.arange(512)[:, None]
    j = np.arange(1024)[None, :]
    m = np.where((j >= i) & (j <= i + 512), 0.0, NEG).astype(np.float32)
    c["lmask"] = m
    return c

CONSTS = _build_constants()


ENGINES = ("tensor", "vector", "scalar", "gpsimd", "sync")
_OWN = {"tD": "tensor", "vD": "vector", "sD": "scalar", "gD": "gpsimd",
        "fS": "sync", "fG": "gpsimd"}

_UNIQ = [0]
def _u(name):
    _UNIQ[0] += 1
    return f"{name}_{_UNIQ[0]}"

class Ktx:
    def __init__(self, nc, es):
        self.nc = nc
        self.es = es
        self.ops = []
        self.sems = {}
        self.cnt = {}
        self.consts_sb = {}
        self.dram_consts = {}
        self.drows = None
        self.cpool = es.enter_context(nc.sbuf_tensor("cpool", [128, 16], F32))
        self.cmap = {}

    def semh(self, name):
        if name not in self.sems:
            self.sems[name] = self.es.enter_context(self.nc.semaphore(name))
            self.cnt[name] = 0
        return self.sems[name]

    def _op(self, eng, fn, waits=(), inc=None, inc_by=1, drain=False):
        ev = None
        if inc is not None:
            self.semh(inc)
            self.cnt[inc] += inc_by
            ev = (inc, self.cnt[inc])
        ws = [w for w in waits if w is not None]
        for w in ws:
            self.semh(w[0])
        self.ops.append((eng, fn, ws, (inc, inc_by) if inc else None, drain))
        return ev

    def T(self, fn, waits=(), inc=False):
        return self._op("tensor", fn, waits, "tD" if inc else None)
    def V(self, fn, waits=(), inc=False, drain=False):
        return self._op("vector", fn, waits, "vD" if inc else None, drain=drain)
    def S(self, fn, waits=(), inc=False, drain=False):
        return self._op("scalar", fn, waits, "sD" if inc else None, drain=drain)
    def G(self, fn, waits=(), inc=False, drain=False):
        return self._op("gpsimd", fn, waits, "gD" if inc else None, drain=drain)

    def ld(self, out, in_, waits=(), queue="s"):
        eng, sem = ("sync", "qS") if queue == "s" else ("gpsimd", "qG")
        fn = lambda e, out=out, in_=in_: e.dma_start(out=out, in_=in_)
        self._op(eng, fn, waits, sem, inc_by=16)

    def dmat(self, out, in_, waits=()):
        fn = lambda e, out=out, in_=in_: e.dma_start_transpose(out, in_)
        self._op("sync", fn, waits, "qS", inc_by=16)

    def flush(self, queue="s"):
        eng, qn, fname = (("sync", "qS", "fS") if queue == "s"
                          else ("gpsimd", "qG", "fG"))
        self.semh(qn)
        tot = self.cnt[qn]
        def fn(e, tot=tot, qn=qn):
            if tot:
                e.wait_ge(self.sems[qn], tot)
            return e.nop()
        return self._op(eng, fn, (), fname)

    def barrier(self):
        self.semh("bar")
        qs_tot = self.cnt.get("qS", 0)
        qg_tot = self.cnt.get("qG", 0)
        target = self.cnt["bar"] + len(ENGINES)
        for e in ENGINES:
            def fn(eng, qs=qs_tot, qg=qg_tot, tgt=target):
                if qs:
                    eng.wait_ge(self.sems["qS"], qs)
                if qg:
                    eng.wait_ge(self.sems["qG"], qg)
                eng.sem_inc(self.sems["bar"], 1)
                eng.wait_ge(self.sems["bar"], tgt)
                return None
            self._op(e, fn)
        self.cnt["bar"] = target

    def cscalar(self, value):
        if value not in self.cmap:
            col = len(self.cmap)
            self.cmap[value] = col
            self.V(lambda v, col=col, value=value: v.memset(self.cpool[:, col:col + 1], value),
                   inc=True, drain=True)
        return self.cpool[:, self.cmap[value]:self.cmap[value] + 1]

    def load_consts(self, names):
        for key in names:
            if key in self.consts_sb:
                continue
            arr = CONSTS[key]
            dt = BF16 if arr.dtype == BF else F32
            h = self.es.enter_context(self.nc.sbuf_tensor("c_" + key, list(arr.shape), dt))
            self.consts_sb[key] = h
            self.ld(h[:], self.dconst(key), queue="s")

    def dconst(self, key):
        if key not in self.dram_consts:
            arr = CONSTS[key]
            dt = BF16 if arr.dtype == BF else F32
            self.dram_consts[key] = self.nc.dram_tensor(
                "const_" + key, list(arr.shape), dt, kind="ExternalInput").ap()
        return self.dram_consts[key]

    def finalize(self):
        blk = self.es.enter_context(self.nc.Block())
        for ename in ENGINES:
            def mk(ename):
                def emit(eng):
                    waited = {}
                    for e, fn, ws, inc, drain in self.ops:
                        if e != ename:
                            continue
                        for (sn, val) in ws:
                            if _OWN.get(sn) == ename:
                                continue
                            if waited.get(sn, -1) >= val:
                                continue
                            eng.wait_ge(self.sems[sn], val)
                            waited[sn] = val
                        inst = fn(eng)
                        if inc is not None:
                            sn, by = inc
                            inst.then_inc(self.sems[sn], by)
                        if drain:
                            eng.drain()
                return emit
            getattr(blk, ename)(mk(ename))


class Ring:
    def __init__(self, n):
        self.ev = [None] * n
        self.n = n
        self.i = -1
    def nxt(self):
        self.i = (self.i + 1) % self.n
        return self.i, self.ev[self.i]
    def done(self, ev):
        self.ev[self.i] = ev



def emit_ln(k, src, dst_bf, dst_f32=None, gamma_beta=None, shift=False, eps=1e-5):
    nc = k.nc
    with contextlib.ExitStack() as ph:
        S = lambda n, sh, dt: ph.enter_context(nc.sbuf_tensor(_u(n), sh, dt))
        xb = [S(f"ln_xb{c}", [128, T], BF16) for c in range(NCH)]
        xq = [S(f"ln_xq{i}", [128, TG], BF16) for i in range(2)]
        rows = S("ln_rows", [1, TG * 4], F32)
        Rb = S("ln_R", [128, TG], F32)
        Mb = S("ln_M", [128, TG], F32)
        gtile = S("ln_g", [128, NCH], F32) if gamma_beta else None
        btile = S("ln_b", [128, NCH], F32) if gamma_beta else None
        ot = [S(f"ln_o{i}", [128, TG], BF16) for i in range(NCH)]
        o32 = [S(f"ln_o32_{i}", [128, TG], F32) for i in range(NCH)] if dst_f32 is not None else None
        ps_s = ph.enter_context(nc.psum_tensor(_u("ln_ps_s"), [1, TG], F32))
        ps_q = ph.enter_context(nc.psum_tensor(_u("ln_ps_q"), [1, TG], F32))
        ones1 = k.consts_sb["ones_bf"]
        drow = k.drows

        if gamma_beta:
            g_dram, b_dram = gamma_beta
            k.ld(gtile[:], g_dram, queue="s")
            k.ld(btile[:], b_dram, queue="s")
        mz = []
        for c in range(NCH):
            if shift and c >= 4:
                k.ld(xb[c][:, 1:T], src[c * 128:(c + 1) * 128, 0:T - 1], queue="g")
                mz.append(k.V(lambda v, c=c: v.memset(xb[c][:, 0:1], 0.0), inc=True))
            else:
                k.ld(xb[c][:], src[c * 128:(c + 1) * 128, :], queue="g")
        lev = [k.flush("g"), k.flush("s")] + mz

        xqr = Ring(2)
        group_flush = None
        stats_read = None
        for g in range(NG):
            gs = slice(g * TG, (g + 1) * TG)
            for c in range(NCH):
                k.T(lambda t, c=c, gs=gs: t.matmul(
                    ps_s[:], ones1[:, 0:1], xb[c][:, gs],
                    start=(c == 0), stop=(c == NCH - 1)),
                    waits=lev + ([stats_read] if c == 0 else []))
            for c in range(NCH):
                qi, qev = xqr.nxt()
                sq = k.S(lambda s, c=c, qi=qi, gs=gs: s.activation(
                    out=xq[qi][:], in_=xb[c][:, gs], func=AF.Square),
                    waits=lev + [qev], inc=True, drain=True)
                mm = k.T(lambda t, qi=qi, c=c: t.matmul(
                    ps_q[:], ones1[:, 0:1], xq[qi][:],
                    start=(c == 0), stop=(c == NCH - 1)), waits=[sq], inc=True)
                xqr.done(mm)
            mu = rows[:, 0:TG]; var = rows[:, TG:2 * TG]
            rstd = rows[:, 2 * TG:3 * TG]; mur = rows[:, 3 * TG:4 * TG]
            k.V(lambda v: v.tensor_scalar(mu, ps_s[:], 1.0 / D, None, op0=ALU.mult),
                waits=[("tD", k.cnt["tD"])], drain=True)
            k.V(lambda v: v.tensor_scalar(var, ps_q[:], 1.0 / D, None, op0=ALU.mult),
                drain=True)
            k.V(lambda v: v.tensor_mul(mur, mu, mu), drain=True)
            e3 = k.V(lambda v: v.tensor_sub(var, var, mur), inc=True, drain=True)
            stats_read = e3
            epsap = k.cscalar(float(eps))
            k.S(lambda s, epsap=epsap: s.activation(out=rstd, in_=var, func=AF.Ln,
                                                    bias=epsap[0:1, :]),
                waits=[e3, ("vD", k.cnt["vD"])], drain=True)
            e5 = k.S(lambda s: s.activation(out=rstd, in_=rstd, func=AF.Exp, scale=-0.5),
                     inc=True, drain=True)
            e6 = k.V(lambda v: v.tensor_mul(mur, mu, rstd), waits=[e5], inc=True, drain=True)
            k.ld(drow[0:1, 0:TG], rstd, waits=[e6], queue="s")
            k.ld(drow[1:2, 0:TG], mur, waits=[e6], queue="s")
            f1 = k.flush("s")
            k.ld(Rb[:], drow[0:1, 0:TG].partition_broadcast(128), waits=[f1], queue="s")
            k.ld(Mb[:], drow[1:2, 0:TG].partition_broadcast(128), waits=[f1], queue="s")
            f2 = k.flush("s")
            for c in range(NCH):
                o = ot[c]
                k.V(lambda v, c=c, o=o, gs=gs: v.tensor_mul(o[:], xb[c][:, gs], Rb[:]),
                    waits=[f2, group_flush], drain=True)
                last = k.V(lambda v, o=o: v.tensor_sub(o[:], o[:], Mb[:]),
                           inc=True, drain=True)
                if gamma_beta:
                    last = k.V(lambda v, c=c, o=o: v.tensor_scalar(
                        o[:], o[:], gtile[:, c:c + 1], btile[:, c:c + 1],
                        op0=ALU.mult, op1=ALU.add), inc=True, drain=True)
                if dst_f32 is not None:
                    o3 = o32[c]
                    c32 = k.V(lambda v, o=o, o3=o3: v.tensor_copy(o3[:], o[:]),
                              inc=True, drain=True)
                    k.ld(dst_f32[c * 128:(c + 1) * 128, gs], o3[:], waits=[c32], queue="s")
                k.ld(dst_bf[c * 128:(c + 1) * 128, gs], o[:], waits=[last], queue="s")
            group_flush = k.flush("s")
    k.barrier()


def _evac_plain(k, ps, st, fw, bias_ap, out_scale, waits):
    if bias_ap is not None and out_scale is not None:
        return k.V(lambda v: v.tensor_scalar(st[0:fw, :], ps[0:fw, :], bias_ap,
                                             out_scale, op0=ALU.add, op1=ALU.mult),
                   waits=waits, inc=True, drain=True)
    if bias_ap is not None:
        return k.V(lambda v: v.tensor_scalar(st[0:fw, :], ps[0:fw, :], bias_ap,
                                             None, op0=ALU.add),
                   waits=waits, inc=True, drain=True)
    if out_scale is not None:
        return k.V(lambda v: v.tensor_scalar(st[0:fw, :], ps[0:fw, :], out_scale,
                                             None, op0=ALU.mult),
                   waits=waits, inc=True, drain=True)
    return k.S(lambda s: s.copy(st[0:fw, :], ps[0:fw, :]),
               waits=waits, inc=True, drain=True)


def emit_proj(k, src_bf, W, F_out, dst=None, bias2d=None, out_scale=None,
              resid=None, resid_out=None, hz2d=None, Din=D, shift=False,
              src_is_f32=False):
    nkc = (Din + 127) // 128
    nsplit = 1
    while nkc * (T // nsplit) * 2 * 128 > 12 * 2 ** 20:
        nsplit *= 2
    for sp in range(nsplit):
        _emit_proj_span(k, src_bf, W, F_out, dst, bias2d, out_scale, resid,
                        resid_out, hz2d, Din, shift, src_is_f32,
                        sp * (T // nsplit), (sp + 1) * (T // nsplit))


def _emit_proj_span(k, src_bf, W, F_out, dst, bias2d, out_scale, resid,
                    resid_out, hz2d, Din, shift, src_is_f32, t0, t1):
    nc = k.nc
    TS = t1 - t0
    ng = TS // TG
    nkc = (Din + 127) // 128
    nf = (F_out + 127) // 128
    with contextlib.ExitStack() as ph:
        S = lambda n, sh, dt: ph.enter_context(nc.sbuf_tensor(_u(n), sh, dt))
        XL = [S(f"pj_x{c}", [128, TS], BF16) for c in range(nkc)]
        WT = [S(f"pj_w{i}", [128, 128 * nkc], BF16) for i in range(2)]
        btile = S("pj_b", [128, nf], F32) if bias2d is not None else None
        hztile = S("pj_hz", [128, nf], F32) if hz2d is not None else None
        stage = [S(f"pj_s{i}", [128, TG], BF16) for i in range(4)]
        rstage = [S(f"pj_r{i}", [128, TG], F32) for i in range(4)]
        xres = [S(f"pj_xr{i}", [128, TG], F32) for i in range(ng)] if resid is not None else None
        pss = [ph.enter_context(nc.psum_tensor(_u("pj_ps"), [128, TG], F32)) for i in range(4)]

        q_ = "g" if src_is_f32 else "s"
        mz = []
        for c in range(nkc):
            rlo, rhi = c * 128, min(Din, (c + 1) * 128)
            if shift and c >= nkc // 2:
                if t0 == 0:
                    k.ld(XL[c][0:rhi - rlo, 1:TS], src_bf[rlo:rhi, 0:TS - 1], queue=q_)
                    mz.append(k.V(lambda v, c=c: v.memset(XL[c][:, 0:1], 0.0), inc=True))
                else:
                    k.ld(XL[c][0:rhi - rlo, :], src_bf[rlo:rhi, t0 - 1:t1 - 1], queue=q_)
            else:
                k.ld(XL[c][0:rhi - rlo, :], src_bf[rlo:rhi, t0:t1], queue=q_)
        if bias2d is not None:
            k.ld(btile[:], bias2d, queue="s")
        if hz2d is not None:
            k.ld(hztile[:], hz2d, queue="s")
        flx = [k.flush("s"), k.flush("g")] + mz

        wring = Ring(2)
        psr = Ring(4)
        str_ = Ring(4)
        Wr = W.rearrange("(c p) f -> p c f", p=128) if Din % 128 == 0 else None
        ftile_flush = None
        for f in range(nf):
            wb, wev_old = wring.nxt()
            flo, fhi = f * 128, min(F_out, (f + 1) * 128)
            fw = fhi - flo
            if Wr is not None:
                k.ld(WT[wb][:, 0:nkc * fw].rearrange("p (c j) -> p c j", j=fw),
                     Wr[:, 0:nkc, flo:fhi], waits=[wev_old], queue="g")
            else:
                for c in range(nkc):
                    kw = min(Din, (c + 1) * 128) - c * 128
                    k.ld(WT[wb][0:kw, c * fw:(c + 1) * fw],
                         W[c * 128:c * 128 + kw, flo:fhi],
                         waits=[wev_old] if c == 0 else (), queue="g")
            wev = k.flush("g")
            xflush = None
            if resid is not None:
                for g in range(ng):
                    k.ld(xres[g][0:fw, :], resid[flo:fhi, t0 + g * TG:t0 + (g + 1) * TG],
                         waits=[ftile_flush], queue="s")
                xflush = k.flush("s")
            f_last_mm = None
            for g in range(ng):
                gs = slice(g * TG, (g + 1) * TG)
                gabs = slice(t0 + g * TG, t0 + (g + 1) * TG)
                pi, pev = psr.nxt()
                ps = pss[pi]
                for c in range(nkc):
                    kw = min(Din, (c + 1) * 128) - c * 128
                    mm = k.T(lambda t, ps=ps, wb=wb, c=c, fw=fw, gs=gs, kw=kw:
                             t.matmul(ps[0:fw, :], WT[wb][0:kw, c * fw:(c + 1) * fw],
                                      XL[c][0:kw, gs], start=(c == 0), stop=(c == nkc - 1)),
                             waits=([wev, pev] + flx) if c == 0 else (),
                             inc=(c == nkc - 1))
                f_last_mm = mm
                si, sev_old = str_.nxt()
                if resid is None:
                    st = stage[si]
                    bap = btile[0:fw, f:f + 1] if bias2d is not None else None
                    e = _evac_plain(k, ps, st, fw, bap, out_scale,
                                    [mm, sev_old, ftile_flush])
                    k.ld(dst[flo:fhi, gabs], st[0:fw, :], waits=[e], queue="s")
                    str_.done(e)
                    psr.done(e)
                else:
                    rt = rstage[si]
                    xr = xres[g]
                    cur = ps
                    if bias2d is not None:
                        k.V(lambda v, rt=rt, ps=ps, f=f, fw=fw:
                            v.tensor_scalar(rt[0:fw, :], ps[0:fw, :],
                                            btile[0:fw, f:f + 1], None, op0=ALU.add),
                            waits=[mm, sev_old, ftile_flush], drain=True)
                        cur = rt
                    if hz2d is not None:
                        k.V(lambda v, rt=rt, cur=cur, f=f, fw=fw:
                            v.tensor_scalar(rt[0:fw, :], cur[0:fw, :],
                                            hztile[0:fw, f:f + 1], None, op0=ALU.mult),
                            waits=[mm, sev_old, ftile_flush], drain=True)
                        cur = rt
                    e = k.V(lambda v, rt=rt, cur=cur, xr=xr, fw=fw:
                            v.tensor_add(rt[0:fw, :], cur[0:fw, :], xr[0:fw, :]),
                            waits=[xflush, mm, sev_old, ftile_flush], inc=True, drain=True)
                    k.ld(resid_out[flo:fhi, gabs], rt[0:fw, :], waits=[e], queue="s")
                    str_.done(e)
                    psr.done(e)
            ftile_flush = k.flush("s")
            wring.done(f_last_mm)
    k.barrier()


def emit_attn(k, qT, kT, vT, ctxT, causal, heads_z2d=None, nheads=16):
    nc = k.nc
    ident = k.consts_sb["ident_bf"]
    ind2 = k.consts_sb["ind2_bf"]
    drow = k.drows
    NT = T // 128

    def _hc_body(hc, ph):
        S = lambda n, sh, dt: ph.enter_context(nc.sbuf_tensor(_u(n), sh, dt))
        if True:
            q2 = S("at_q2", [128, T], BF16)
            k2 = S("at_k2", [128, T], BF16)
            v2 = S("at_v2", [128, T], BF16)
            vtr = [S(f"at_vtr{h}", [128, NT * 65], BF16) for h in range(2)]
            sq = S("at_sq", [128, T], BF16)
            nrm = S("at_nrm", [2, NG], F32)
            brow = S("at_brow", [2, 4], F32)
            negb = [S(f"at_negb{h}", [128, 1], F32) for h in range(2)]
            hzb = [S(f"at_hz{h}", [64, 1], F32) for h in range(2)] if heads_z2d is not None else None
            pT = [S(f"at_pT{h}", [128, NT * TG], BF16) for h in range(2)]
            recr = S("at_rec", [128, TG], F32)
            recb = [S(f"at_recb{h}", [64, TG], F32) for h in range(2)]
            sctx = [S(f"at_sctx{h}", [64, TG], BF16) for h in range(2)]
            octx = [S(f"at_octx{h}", [64, TG], BF16) for h in range(2)]
            ps_tr = ph.enter_context(nc.psum_tensor(_u("at_ps_tr"), [128, 64], BF16))
            ps_n = ph.enter_context(nc.psum_tensor(_u("at_ps_n"), [2, TG], F32))
            ps_s = [ph.enter_context(nc.psum_tensor(_u("at_ps_s"), [128, TG], F32)) for i in range(4)]
            ps_c = [ph.enter_context(nc.psum_tensor(_u("at_ps_c"), [65, TG], F32)) for i in range(2)]

            r0 = hc * 128
            k.ld(q2[:], qT[r0:r0 + 128, :], queue="s")
            k.ld(k2[:], kT[r0:r0 + 128, :], queue="s")
            k.ld(v2[:], vT[r0:r0 + 128, :], queue="s")
            fl0 = k.flush("s")

            mevs = [k.V(lambda v, h=h: v.memset(vtr[h][:], 1.0), inc=True) for h in range(2)]
            prev_evac = None
            for kt in range(NT):
                for h in range(2):
                    tre = k.T(lambda t, kt=kt, h=h: t.transpose(
                        ps_tr[:, 0:64], v2[h * 64:(h + 1) * 64, kt * 128:(kt + 1) * 128],
                        ident[h * 64:(h + 1) * 64, h * 64:(h + 1) * 64]),
                        waits=[fl0, prev_evac] + mevs, inc=True)
                    prev_evac = k.S(lambda s, kt=kt, h=h: s.copy(
                        vtr[h][:, kt * 65:kt * 65 + 64], ps_tr[:, 0:64]),
                        waits=[tre] + mevs, inc=True, drain=True)
            vtr_ev = prev_evac

            def norms(srcbuf, col, extra_wait):
                sqe = k.S(lambda s: s.activation(out=sq[:], in_=srcbuf[:], func=AF.Square),
                          waits=[fl0, extra_wait], inc=True, drain=True)
                rd = None
                last_mm = None
                for g in range(NG):
                    me = k.T(lambda t, g=g: t.matmul(
                        ps_n[:, :], ind2[:], sq[:, g * TG:(g + 1) * TG],
                        start=True, stop=True), waits=[sqe, rd], inc=True)
                    last_mm = me
                    rd = k.V(lambda v, g=g: v.reduce_max(nrm[:, g:g + 1], ps_n[:], axis=AX),
                             waits=[me], inc=True, drain=True)
                e = k.V(lambda v, col=col: v.reduce_max(brow[:, col:col + 1], nrm[:], axis=AX),
                        waits=[rd], inc=True, drain=True)
                return e, last_mm
            eq_n, qmm = norms(q2, 0, None)
            ek_n, _ = norms(k2, 1, qmm)
            e_sq = k.S(lambda s: s.activation(out=brow[:, 0:2], in_=brow[:, 0:2], func=AF.Sqrt),
                       waits=[eq_n, ek_n], inc=True, drain=True)
            k.V(lambda v: v.tensor_mul(brow[:, 2:3], brow[:, 0:1], brow[:, 1:2]),
                waits=[e_sq], drain=True)
            e_b = k.V(lambda v: v.tensor_scalar(brow[:, 3:4], brow[:, 2:3], -1.0, None,
                                                op0=ALU.mult), inc=True, drain=True)
            k.ld(drow[2:3, 0:2], brow[:, 3:4], waits=[e_b], queue="s")
            fb1 = k.flush("s")
            for h in range(2):
                k.ld(negb[h][:], drow[2:3, h:h + 1].partition_broadcast(128),
                     waits=[fb1], queue="s")
            if heads_z2d is not None:
                for h in range(2):
                    k.ld(hzb[h][:], heads_z2d[0:1, hc * 2 + h:hc * 2 + h + 1]
                         .partition_broadcast(64), queue="s")
            fb2 = k.flush("s")

            ssr = Ring(4)
            pvr = [None, None]
            octx_ok = [None, None]
            for qc in range(NG):
                qs_ = slice(qc * TG, (qc + 1) * TG)
                ktmax = NT if not causal else (qc + 1) * (TG // 128)
                exp_done = [None, None]
                for h in range(2):
                    for kt in range(ktmax):
                        pi, pev = ssr.nxt()
                        ps = ps_s[pi]
                        me = k.T(lambda t, ps=ps, h=h, kt=kt, qs_=qs_: t.matmul(
                            ps[:], k2[h * 64:(h + 1) * 64, kt * 128:(kt + 1) * 128],
                            q2[h * 64:(h + 1) * 64, qs_], start=True, stop=True),
                            waits=[fl0, pev, pvr[h]] if kt == 0 else [pev],
                            inc=True)
                        ee = k.S(lambda s, ps=ps, h=h, kt=kt: s.activation(
                            out=pT[h][:, kt * TG:(kt + 1) * TG], in_=ps[:], func=AF.Exp,
                            bias=negb[h][:]), waits=[me, fb2], inc=True, drain=True)
                        if causal and kt >= qc * (TG // 128):
                            ee = k.G(lambda g, h=h, kt=kt, qc=qc: g.affine_select(
                                out=pT[h][:, kt * TG:(kt + 1) * TG],
                                in_=pT[h][:, kt * TG:(kt + 1) * TG],
                                pattern=[[1, TG]], compare_op=ALU.is_ge, fill=0.0,
                                base=qc * TG - kt * 128, channel_multiplier=-1),
                                waits=[ee], inc=True, drain=True)
                        ssr.done(ee)
                        exp_done[h] = ee
                for h in range(2):
                    last_pv = None
                    for kt in range(ktmax):
                        last_pv = k.T(lambda t, h=h, kt=kt, ktmax=ktmax: t.matmul(
                            ps_c[h][:], vtr[h][:, kt * 65:(kt + 1) * 65],
                            pT[h][:, kt * TG:(kt + 1) * TG],
                            start=(kt == 0), stop=(kt == ktmax - 1)),
                            waits=[exp_done[h], octx_ok[h], vtr_ev] if kt == 0 else (),
                            inc=(kt == ktmax - 1))
                    pvr[h] = last_pv
                    re = k.V(lambda v, h=h: v.reciprocal(recr[64:65, :], ps_c[h][64:65, :]),
                             waits=[last_pv], inc=True, drain=True)
                    k.ld(drow[3 + h:4 + h, 0:TG], recr[64:65, :], waits=[re], queue="s")
                    fr1 = k.flush("s")
                    k.ld(recb[h][:], drow[3 + h:4 + h, 0:TG].partition_broadcast(64),
                         waits=[fr1], queue="s")
                    fr2 = k.flush("s")
                    ce = k.S(lambda s, h=h: s.copy(sctx[h][:], ps_c[h][0:64, :]),
                             waits=[last_pv], inc=True, drain=True)
                    me1 = k.V(lambda v, h=h: v.tensor_mul(octx[h][:], sctx[h][:], recb[h][:]),
                              waits=[fr2, ce, octx_ok[h]], inc=True, drain=True)
                    if heads_z2d is not None:
                        me1 = k.V(lambda v, h=h: v.tensor_scalar(
                            octx[h][:], octx[h][:], hzb[h][:], None, op0=ALU.mult),
                            inc=True, drain=True)
                    k.ld(ctxT[r0 + h * 64:r0 + (h + 1) * 64, qs_], octx[h][:],
                         waits=[me1], queue="s")
                    octx_ok[h] = k.flush("s")

    for hc in range(nheads // 2):
        with contextlib.ExitStack() as ph:
            _hc_body(hc, ph)
        k.barrier()


def emit_rope_il(k, bufT):
    nc = k.nc
    rot = k.consts_sb["rot_il_bf"]
    with contextlib.ExitStack() as ph:
        S = lambda n, sh, dt: ph.enter_context(nc.sbuf_tensor(_u(n), sh, dt))
        cosT = S("rp_cos", [128, T], F32)
        sinT = S("rp_sin", [128, T], F32)
        tq = [S(f"rp_q{i}", [128, T], BF16) for i in range(2)]
        to = [S(f"rp_o{i}", [128, T], BF16) for i in range(2)]
        t1 = S("rp_t1", [128, TG], F32)
        t2 = S("rp_t2", [128, TG], F32)
        pss = [ph.enter_context(nc.psum_tensor(_u("rp_ps"), [128, TG], F32)) for i in range(2)]
        k.ld(cosT[:], k.dconst("ropeA_cos"), queue="s")
        k.ld(sinT[:], k.dconst("ropeA_sin"), queue="s")
        psr = Ring(2)
        chunk_flush = [None, None]
        for c in range(NCH):
            bi = c % 2
            k.ld(tq[bi][:], bufT[c * 128:(c + 1) * 128, :], waits=[chunk_flush[bi]], queue="s")
            fl = k.flush("s")
            last = None
            for g in range(NG):
                gs = slice(g * TG, (g + 1) * TG)
                pi, pev = psr.nxt()
                ps = pss[pi]
                me = k.T(lambda t, ps=ps, bi=bi, gs=gs: t.matmul(
                    ps[:], rot[:], tq[bi][:, gs], start=True, stop=True),
                    waits=[fl, pev], inc=True)
                k.V(lambda v, bi=bi, gs=gs: v.tensor_mul(t1[:], tq[bi][:, gs], cosT[:, gs]),
                    waits=[fl, ("vD", k.cnt["vD"])], drain=True)
                k.V(lambda v, ps=ps, gs=gs: v.tensor_mul(t2[:], ps[:], sinT[:, gs]),
                    waits=[me], drain=True)
                last = k.V(lambda v, bi=bi, gs=gs: v.tensor_add(to[bi][:, gs], t1[:], t2[:]),
                           waits=[chunk_flush[bi]], inc=True, drain=True)
                psr.done(last)
            k.ld(bufT[c * 128:(c + 1) * 128, :], to[bi][:], waits=[last], queue="s")
            chunk_flush[bi] = k.flush("s")
    k.barrier()


MOE_DBG = [None]

def emit_moe(k, xln_bf, rwT, EW, xin, xout):
    nc = k.nc
    ident = k.consts_sb["ident_f32"]
    NT = T // 128
    with contextlib.ExitStack() as ph:
        S = lambda n, sh, dt: ph.enter_context(nc.sbuf_tensor(_u(n), sh, dt))
        Wsb = S("mo_w", [128, 8 * 2 * 8 * 512], BF16)
        rw = S("mo_rw", [128, 8 * 8], BF16)
        xlt = [S(f"mo_x{i}", [128, 8 * 128], BF16) for i in range(2)]
        lgtb = S("mo_lg", [128, 48 * NT], F32)
        srowb = S("mo_sr", [128, 8 * NT], F32)
        acc = [S(f"mo_acc{i}", [128, 1024], F32) for i in range(2)]
        xr = [S(f"mo_xr{i}", [128, 128], F32) for i in range(8)]
        rt = [S(f"mo_rt{i}", [128, 128], F32) for i in range(8)]
        ps_e = [ph.enter_context(nc.psum_tensor(_u("mo_ps"), [128, TG], F32)) for i in range(8)]

        for e in range(8):
            for fg in range(2):
                base = (e * 2 + fg) * 8 * 512
                k.ld(Wsb[:, base:base + 8 * 512].rearrange("p (c j) -> p c j", j=512),
                     EW[e].rearrange("(c p) f -> p c f", p=128)[:, :, fg * 512:(fg + 1) * 512],
                     queue="g")
        k.ld(rw[:].rearrange("p (c e) -> p c e", e=8),
             rwT.rearrange("(c p) e -> p c e", p=128), queue="g")
        wfl = k.flush("g")

        xlr = Ring(2)
        tile_flush = None
        for tt in range(NT):
            ts_ = slice(tt * 128, (tt + 1) * 128)
            xi, xev_old = xlr.nxt()
            xl = xlt[xi]
            for c in range(NCH):
                k.ld(xl[:, c * 128:(c + 1) * 128], xln_bf[c * 128:(c + 1) * 128, ts_],
                     waits=[xev_old], queue="s")
            for j in range(8):
                k.ld(xr[j][:], xin[j * 128:(j + 1) * 128, ts_],
                     waits=[tile_flush], queue="s")
            lfl = k.flush("s")
            ps_r = ps_e[0][:, 0:8]
            lgt = lgtb[:, tt * 48:(tt + 1) * 48]
            srow = srowb[:, tt * 8:(tt + 1) * 8]
            rme = None
            vs = ("vD", k.cnt["vD"])
            for c in range(NCH):
                rme = k.T(lambda t, c=c, xl=xl: t.matmul(
                    ps_r, xl[:, c * 128:(c + 1) * 128], rw[:, c * 8:(c + 1) * 8],
                    start=(c == 0), stop=(c == NCH - 1)),
                    waits=[lfl, wfl, vs] if c == 0 else (), inc=True)
            l8 = lgt[:, 0:8]; p8 = lgt[:, 8:16]; eq8 = lgt[:, 16:24]
            q8 = lgt[:, 24:32]; ge8 = lgt[:, 32:40]; cf8 = lgt[:, 40:48]
            negm = srow[:, 0:1]; esum = srow[:, 1:2]; rec = srow[:, 2:3]
            M1 = srow[:, 3:4]; M2 = srow[:, 4:5]; s12 = srow[:, 5:6]; beta = srow[:, 6:7]
            k.V(lambda v, l8=l8, ps_r=ps_r: v.tensor_copy(l8, ps_r), waits=[rme], drain=True)
            nme = k.V(lambda v, negm=negm, l8=l8: v.reduce_max(negm, l8, axis=AX, negate=True),
                      inc=True, drain=True)
            se = k.S(lambda s, p8=p8, l8=l8, negm=negm, esum=esum: s.activation(
                     out=p8, in_=l8, func=AF.Exp, bias=negm, accum_out=esum),
                     waits=[nme], inc=True, drain=True)
            k.V(lambda v, rec=rec, esum=esum: v.reciprocal(rec, esum), waits=[se], drain=True)
            k.V(lambda v, p8=p8, rec=rec: v.tensor_scalar(p8, p8, rec, None, op0=ALU.mult), drain=True)
            k.V(lambda v, M1=M1, p8=p8: v.reduce_max(M1, p8, axis=AX), drain=True)
            k.V(lambda v, eq8=eq8, p8=p8, M1=M1: v.tensor_scalar(eq8, p8, M1, None, op0=ALU.is_equal), drain=True)
            k.V(lambda v, q8=q8, eq8=eq8, p8=p8: v.scalar_tensor_tensor(q8, eq8, -1e30, p8, op0=ALU.mult, op1=ALU.add), drain=True)
            k.V(lambda v, M2=M2, q8=q8: v.reduce_max(M2, q8, axis=AX), drain=True)
            k.V(lambda v, ge8=ge8, p8=p8, M2=M2: v.tensor_scalar(ge8, p8, M2, None, op0=ALU.is_ge), drain=True)
            k.V(lambda v, s12=s12, M1=M1, M2=M2: v.tensor_add(s12, M1, M2), drain=True)
            k.V(lambda v, s12=s12: v.tensor_scalar(s12, s12, 1e-6, None, op0=ALU.add), drain=True)
            k.V(lambda v, rec=rec, s12=s12: v.reciprocal(rec, s12), drain=True)
            k.V(lambda v, cf8=cf8, p8=p8, ge8=ge8: v.tensor_mul(cf8, p8, ge8), drain=True)
            k.V(lambda v, cf8=cf8, rec=rec: v.tensor_scalar(cf8, cf8, rec, None, op0=ALU.mult), drain=True)
            k.V(lambda v, q8=q8, cf8=cf8, p8=p8: v.tensor_mul(q8, cf8, p8), drain=True)
            ce = k.V(lambda v, beta=beta, q8=q8: v.reduce_sum(beta, q8, axis=AX), inc=True, drain=True)
            accev = [None, None]
            emms = []
            for fg in range(2):
                ac = acc[fg]
                emms = []
                vs2 = ("vD", k.cnt["vD"])
                for c in range(NCH):
                    for e in range(8):
                        base = (e * 2 + fg) * 8 * 512
                        me = k.T(lambda t, c=c, e=e, base=base, xl=xl: t.matmul(
                            ps_e[e][:], xl[:, c * 128:(c + 1) * 128],
                            Wsb[:, base + c * 512:base + (c + 1) * 512],
                            start=(c == 0), stop=(c == NCH - 1)),
                            waits=[vs2, ce] if c == 0 else (), inc=(c == NCH - 1))
                        if c == NCH - 1:
                            emms.append(me)
                k.V(lambda v, ac=ac, cf8=cf8: v.tensor_scalar(
                    ac[:, 0:512], ps_e[0][:], cf8[:, 0:1], None, op0=ALU.mult),
                    waits=[emms[0], ce, tile_flush], drain=True)
                for e in range(1, 8):
                    k.V(lambda v, e=e, ac=ac, cf8=cf8: v.scalar_tensor_tensor(
                        ac[:, 0:512], ps_e[e][:], cf8[:, e:e + 1], ac[:, 0:512],
                        op0=ALU.mult, op1=ALU.add),
                        waits=[emms[e]], drain=True)
                accev[fg] = k.V(lambda v, ac=ac, beta=beta: v.tensor_scalar(
                    ac[:, 0:512], ac[:, 0:512], beta, None, op0=ALU.add),
                    inc=True, drain=True)
            if MOE_DBG[0] is not None:
                k.ld(MOE_DBG[0][:, tt * 56:tt * 56 + 48], lgt[:], waits=[ce], queue="s")
                k.ld(MOE_DBG[0][:, tt * 56 + 48:tt * 56 + 56], srow[:], waits=[ce], queue="s")
            prev_ae = [None, None]
            for j in range(8):
                fg, jj = divmod(j, 4)
                ac = acc[fg]
                pst = ps_e[1 + (j % 2)]
                te = k.T(lambda t, ac=ac, jj=jj, pst=pst: t.transpose(
                    pst[0:128, 0:128], ac[:, jj * 128:(jj + 1) * 128], ident[:]),
                    waits=[accev[1], prev_ae[j % 2]], inc=True)
                ae = k.V(lambda v, j=j, pst=pst: v.tensor_add(
                    rt[j][:], pst[0:128, 0:128], xr[j][:]),
                    waits=[te, lfl, tile_flush], inc=True, drain=True)
                prev_ae[j % 2] = ae
                k.ld(xout[j * 128:(j + 1) * 128, ts_], rt[j][:], waits=[ae], queue="s")
            tile_flush = k.flush("s")
            xlr.done(emms[-1])
    k.barrier()


def emit_attn_b(k, qkvT, xin, xout):
    nc = k.nc
    ind8 = k.consts_sb["indrep8_bf"]
    with contextlib.ExitStack() as ph:
        S = lambda n, sh, dt: ph.enter_context(nc.sbuf_tensor(_u(n), sh, dt))
        qc_ = S("ab_q", [128, T], BF16)
        kc_ = S("ab_k", [128, T], BF16)
        vc_ = S("ab_v", [128, T], BF16)
        xr = S("ab_xr", [128, T], F32)
        Cs = S("ab_C", [128, T], F32)
        zb = S("ab_z", [128, T], F32)
        iot = S("ab_iot", [128, T], F32)
        qkb = [S(f"ab_qk{i}", [128, TG], BF16) for i in range(2)]
        ab = S("ab_a", [128, TG], F32)
        bb = S("ab_b", [128, TG], F32)
        db = S("ab_d", [128, TG], F32)
        rb = S("ab_r", [128, TG], F32)
        nb = S("ab_n", [128, TG], F32)
        t2 = S("ab_t2", [128, TG], F32)
        ob = [S(f"ab_o{i}", [128, TG], F32) for i in range(4)]
        pss = [ph.enter_context(nc.psum_tensor(_u("ab_ps"), [128, TG], F32)) for i in range(2)]

        ze = k.V(lambda v: v.memset(zb[:], 0.0), inc=True, drain=True)
        ie = k.G(lambda g: g.iota(iot[:], pattern=[[1, T]], base=0, channel_multiplier=0,
                                  allow_small_or_imprecise_dtypes=True),
                 inc=True, drain=True)
        chunk_flush = None
        for c in range(NCH):
            k.ld(qc_[:], qkvT[c * 128:(c + 1) * 128, :], waits=[chunk_flush], queue="s")
            k.ld(kc_[:], qkvT[1024 + c * 128:1024 + (c + 1) * 128, :], queue="s")
            k.ld(vc_[:], qkvT[2048 + c * 128:2048 + (c + 1) * 128, :], queue="s")
            k.ld(xr[:], xin[c * 128:(c + 1) * 128, :], queue="s")
            fl = k.flush("s")
            sce = k.V(lambda v: v.tensor_tensor_scan(Cs[:], vc_[:], zb[:], 0.0,
                                                     op0=ALU.add, op1=ALU.bypass),
                      waits=[fl, ze, chunk_flush], inc=True, drain=True)
            qkr = Ring(2)
            psr = Ring(2)
            str_ = Ring(4)
            for g in range(NG):
                gs = slice(g * TG, (g + 1) * TG)
                qi, qev = qkr.nxt()
                qke = k.V(lambda v, qi=qi, gs=gs: v.tensor_mul(qkb[qi][:], qc_[:, gs], kc_[:, gs]),
                          waits=[fl, qev], inc=True, drain=True)
                pi, pev = psr.nxt()
                ps = pss[pi]
                me = k.T(lambda t, ps=ps, qi=qi: t.matmul(
                    ps[:], ind8[:], qkb[qi][:], start=True, stop=True),
                    waits=[qke, pev], inc=True)
                qkr.done(me)
                k.V(lambda v, ps=ps: v.tensor_scalar(ab[:], ps[:], 0.0, None, op0=ALU.min),
                    waits=[me, ("sD", k.cnt["sD"])], drain=True)
                ea = k.V(lambda v, ps=ps: v.tensor_scalar(bb[:], ps[:], -1.0, 0.0,
                                                          op0=ALU.mult, op1=ALU.min),
                         inc=True, drain=True)
                eb = k.S(lambda s: s.activation(out=ab[:], in_=ab[:], func=AF.Exp),
                         waits=[ea], drain=True)
                eb = k.S(lambda s: s.activation(out=bb[:], in_=bb[:], func=AF.Exp),
                         inc=True, drain=True)
                k.V(lambda v, gs=gs: v.tensor_mul(db[:], iot[:, gs], bb[:]),
                    waits=[eb, ie], drain=True)
                k.V(lambda v: v.tensor_add(db[:], db[:], ab[:]), drain=True)
                k.V(lambda v: v.reciprocal(rb[:], db[:]), drain=True)
                k.V(lambda v: v.tensor_sub(ab[:], ab[:], bb[:]), drain=True)
                k.V(lambda v, gs=gs: v.tensor_mul(nb[:], ab[:], vc_[:, gs]), drain=True)
                k.V(lambda v, gs=gs: v.tensor_mul(t2[:], bb[:], Cs[:, gs]),
                    waits=[sce], drain=True)
                k.V(lambda v: v.tensor_add(nb[:], nb[:], t2[:]), drain=True)
                si, sev = str_.nxt()
                o = ob[si]
                k.V(lambda v, o=o: v.tensor_mul(o[:], nb[:], rb[:]),
                    waits=[sev], drain=True)
                oe = k.V(lambda v, o=o, gs=gs: v.tensor_add(o[:], o[:], xr[:, gs]),
                         inc=True, drain=True)
                k.ld(xout[c * 128:(c + 1) * 128, gs], o[:], waits=[oe], queue="s")
                str_.done(None)
            chunk_flush = k.flush("s")
    k.barrier()


def emit_l2rope(k, bufT, row0, scale2d, cos_key, sin_key):
    nc = k.nc
    indrep = k.consts_sb["indrep_bf"]
    rot = k.consts_sb["rot_hs_bf"]
    with contextlib.ExitStack() as ph:
        S = lambda n, sh, dt: ph.enter_context(nc.sbuf_tensor(_u(n), sh, dt))
        cosT = S("lr_cos", [128, T], F32)
        sinT = S("lr_sin", [128, T], F32)
        tq = [S(f"lr_q{i}", [128, T], BF16) for i in range(2)]
        to = [S(f"lr_o{i}", [128, T], BF16) for i in range(2)]
        sqb = S("lr_sq", [128, TG], BF16)
        nbuf = S("lr_n", [128, TG], F32)
        qhb = S("lr_qh", [128, TG], BF16)
        t1 = S("lr_t1", [128, TG], F32)
        t2 = S("lr_t2", [128, TG], F32)
        qs1 = S("lr_qs", [128, 1], F32)
        ps_n = ph.enter_context(nc.psum_tensor(_u("lr_psn"), [128, TG], F32))
        ps_r = ph.enter_context(nc.psum_tensor(_u("lr_psr"), [128, TG], F32))
        k.ld(cosT[:], k.dconst(cos_key), queue="s")
        k.ld(sinT[:], k.dconst(sin_key), queue="s")
        k.ld(qs1[:], scale2d, queue="s")
        chunk_flush = [None, None]
        for c in range(4):
            bi = c % 2
            k.ld(tq[bi][:], bufT[row0 + c * 128:row0 + (c + 1) * 128, :],
                 waits=[chunk_flush[bi]], queue="s")
            fl = k.flush("s")
            last = None
            nme_prev = None
            rme_prev = None
            for g in range(NG):
                gs = slice(g * TG, (g + 1) * TG)
                sqe = k.S(lambda s, bi=bi, gs=gs: s.activation(out=sqb[:], in_=tq[bi][:, gs],
                                                               func=AF.Square),
                          waits=[fl, nme_prev], inc=True, drain=True)
                nme = k.T(lambda t: t.matmul(ps_n[:], indrep[:], sqb[:], start=True, stop=True),
                          waits=[sqe, ("vD", k.cnt["vD"])], inc=True)
                nme_prev = nme
                k.V(lambda v: v.tensor_scalar(nbuf[:], ps_n[:], 1e-24, None, op0=ALU.max),
                    waits=[nme, ("sD", k.cnt["sD"])], drain=True)
                e1 = k.V(lambda v: v.engine_nop(), inc=True)
                k.S(lambda s: s.activation(out=nbuf[:], in_=nbuf[:], func=AF.Ln),
                    waits=[e1], drain=True)
                e2 = k.S(lambda s: s.activation(out=nbuf[:], in_=nbuf[:], func=AF.Exp,
                                                scale=-0.5), inc=True, drain=True)
                k.V(lambda v, bi=bi, gs=gs: v.tensor_mul(t1[:], tq[bi][:, gs], nbuf[:]),
                    waits=[e2], drain=True)
                e3 = k.V(lambda v: v.tensor_scalar(qhb[:], t1[:], qs1[:], None, op0=ALU.mult),
                         waits=[rme_prev], inc=True, drain=True)
                rme = k.T(lambda t: t.matmul(ps_r[:], rot[:], qhb[:], start=True, stop=True),
                          waits=[e3], inc=True)
                rme_prev = rme
                k.V(lambda v, gs=gs: v.tensor_mul(t1[:], qhb[:], cosT[:, gs]),
                    waits=[e3], drain=True)
                k.V(lambda v, gs=gs: v.tensor_mul(t2[:], ps_r[:], sinT[:, gs]),
                    waits=[rme], drain=True)
                last = k.V(lambda v, bi=bi, gs=gs: v.tensor_add(to[bi][:, gs], t1[:], t2[:]),
                           waits=[chunk_flush[bi]], inc=True, drain=True)
            k.ld(bufT[row0 + c * 128:row0 + (c + 1) * 128, :], to[bi][:],
                 waits=[last], queue="s")
            chunk_flush[bi] = k.flush("s")
    k.barrier()


def emit_lattn(k, qkvT, outT):
    nc = k.nc
    ident = k.consts_sb["ident_bf"]
    NT = T // 128
    NW = T // 512
    with contextlib.ExitStack() as ph:
        S = lambda n, sh, dt: ph.enter_context(nc.sbuf_tensor(_u(n), sh, dt))
        msk = S("la_msk", [128, 4 * 1024], F32)
        k.ld(msk[:].rearrange("p (qt j) -> p qt j", j=1024),
             k.dconst("lmask").rearrange("(qt p) j -> p qt j", p=128), queue="s")
        mfl = k.flush("s")
        def _hc_body(hc, ph2):
            S2 = lambda n, sh, dt: ph2.enter_context(nc.sbuf_tensor(_u(n), sh, dt))
            q2 = S2("la_q2", [128, T], BF16)
            k2 = S2("la_k2", [128, T], BF16)
            v2 = S2("la_v2", [128, T], BF16)
            vtr = [S2(f"la_vtr{h}", [128, NT * 64], BF16) for h in range(2)]
            Pb = [S2(f"la_P{h}", [128, 2 * 1024], BF16) for h in range(2)]
            pTt = [S2(f"la_pT{h}", [128, 2 * 1024], BF16) for h in range(2)]
            rr = S2("la_rr", [128, 8], F32)
            octb = S2("la_octb", [128, T], BF16)
            ps_tr = [ph2.enter_context(nc.psum_tensor(_u("la_ptr"), [128, 128], BF16)) for i in range(2)]
            ps_s = [ph2.enter_context(nc.psum_tensor(_u("la_ps"), [128, TG], F32)) for i in range(4)]
            ps_c = [ph2.enter_context(nc.psum_tensor(_u("la_pc"), [64, 128], F32)) for i in range(2)]

            k.ld(q2[:], qkvT[hc * 128:(hc + 1) * 128, :], queue="s")
            k.ld(k2[:], qkvT[512 + hc * 128:512 + (hc + 1) * 128, :], queue="s")
            k.ld(v2[:], qkvT[1024 + hc * 128:1024 + (hc + 1) * 128, :], queue="s")
            fl0 = k.flush("s")
            trr = [None, None]
            for kt in range(NT):
                for h in range(2):
                    b = h
                    tre = k.T(lambda t, kt=kt, h=h, b=b: t.transpose(
                        ps_tr[b][:, 0:64], v2[h * 64:(h + 1) * 64, kt * 128:(kt + 1) * 128],
                        ident[h * 64:(h + 1) * 64, h * 64:(h + 1) * 64]),
                        waits=[fl0, trr[b]], inc=True)
                    trr[b] = k.S(lambda s, kt=kt, h=h, b=b: s.copy(
                        vtr[h][:, kt * 64:(kt + 1) * 64], ps_tr[b][:, 0:64]),
                        waits=[tre], inc=True, drain=True)

            psfree = [[None, None], [None, None]]
            exp_ev = [None, None]
            mul_ev = [None, None]
            pb_ev = [[None, None], [None, None]]
            pt_ev = [[None, None], [None, None]]
            pc_ev = [None, None]
            for w in range(NW):
                for qt in range(4):
                    qlo = w * 512 + qt * 128
                    qsl = slice(qlo, qlo + 128)
                    par = (w * 4 + qt) % 2
                    kt0 = 0 if w > 0 else 4
                    mms = {}
                    for kh in range(2):
                        klo = (w - 1) * 512 + kh * 512
                        if w == 0 and kh == 0:
                            continue
                        for h in range(2):
                            ps = ps_s[h * 2 + kh]
                            mms[(h, kh)] = k.T(lambda t, ps=ps, h=h, qsl=qsl, klo=klo:
                                t.matmul(ps[:], q2[h * 64:(h + 1) * 64, qsl],
                                         k2[h * 64:(h + 1) * 64, klo:klo + 512],
                                         start=True, stop=True),
                                waits=[fl0, psfree[h][kh]], inc=True)
                    madd = {}
                    for (h, kh), me in mms.items():
                        madd[(h, kh)] = k.V(lambda v, h=h, kh=kh, qt=qt:
                            v.tensor_add(ps_s[h * 2 + kh][:], ps_s[h * 2 + kh][:],
                                         msk[:, qt * 1024 + kh * 512:qt * 1024 + (kh + 1) * 512]),
                            waits=[me, mfl, exp_ev[h]], inc=True, drain=True)
                    for h in range(2):
                        negm = rr[:, h * 4:h * 4 + 1]
                        r2_ = rr[:, h * 4 + 1:h * 4 + 2]
                        ssum = rr[:, h * 4 + 2:h * 4 + 3]
                        s2_ = rr[:, h * 4 + 3:h * 4 + 4]
                        pof = par * 1024
                        e0 = None
                        if w == 0:
                            k.V(lambda v, h=h, negm=negm: v.reduce_max(
                                negm, ps_s[h * 2 + 1][:], axis=AX, negate=True),
                                waits=[madd[(h, 1)], exp_ev[h]], drain=True)
                            mne = k.V(lambda v: v.engine_nop(), inc=True)
                            ee = k.S(lambda s, h=h, negm=negm, ssum=ssum, pof=pof: s.activation(
                                out=Pb[h][:, pof + 512:pof + 1024], in_=ps_s[h * 2 + 1][:],
                                func=AF.Exp, bias=negm, accum_out=ssum),
                                waits=[mne, pb_ev[h][par], mul_ev[h]], inc=True, drain=True)
                            psfree[h][1] = ee
                            exp_ev[h] = ee
                            k.V(lambda v, ssum=ssum, s2_=s2_: v.reciprocal(s2_, ssum),
                                waits=[ee], drain=True)
                            ne = k.V(lambda v, h=h, s2_=s2_, pof=pof: v.tensor_scalar(
                                Pb[h][:, pof + 512:pof + 1024], Pb[h][:, pof + 512:pof + 1024],
                                s2_, None, op0=ALU.mult), inc=True, drain=True)
                            mul_ev[h] = ne
                        else:
                            k.V(lambda v, h=h, negm=negm: v.reduce_max(
                                negm, ps_s[h * 2 + 0][:], axis=AX, negate=True),
                                waits=[madd[(h, 0)], exp_ev[h]], drain=True)
                            k.V(lambda v, h=h, r2_=r2_: v.reduce_max(
                                r2_, ps_s[h * 2 + 1][:], axis=AX, negate=True),
                                waits=[madd[(h, 1)]], drain=True)
                            mne = k.V(lambda v, negm=negm, r2_=r2_: v.tensor_tensor(
                                negm, negm, r2_, op=ALU.min), inc=True, drain=True)
                            e0 = k.S(lambda s, h=h, negm=negm, ssum=ssum, pof=pof: s.activation(
                                out=Pb[h][:, pof:pof + 512], in_=ps_s[h * 2 + 0][:],
                                func=AF.Exp, bias=negm, accum_out=ssum),
                                waits=[mne, pb_ev[h][par], mul_ev[h]], inc=True, drain=True)
                            psfree[h][0] = e0
                            ee = k.S(lambda s, h=h, negm=negm, s2_=s2_, pof=pof: s.activation(
                                out=Pb[h][:, pof + 512:pof + 1024], in_=ps_s[h * 2 + 1][:],
                                func=AF.Exp, bias=negm, accum_out=s2_),
                                inc=True, drain=True)
                            psfree[h][1] = ee
                            exp_ev[h] = ee
                            k.V(lambda v, ssum=ssum, s2_=s2_: v.tensor_add(ssum, ssum, s2_),
                                waits=[ee], drain=True)
                            k.V(lambda v, ssum=ssum, s2_=s2_: v.reciprocal(s2_, ssum),
                                drain=True)
                            k.V(lambda v, h=h, s2_=s2_, pof=pof: v.tensor_scalar(
                                Pb[h][:, pof:pof + 512], Pb[h][:, pof:pof + 512], s2_, None,
                                op0=ALU.mult), drain=True)
                            ne = k.V(lambda v, h=h, s2_=s2_, pof=pof: v.tensor_scalar(
                                Pb[h][:, pof + 512:pof + 1024], Pb[h][:, pof + 512:pof + 1024],
                                s2_, None, op0=ALU.mult), inc=True, drain=True)
                            mul_ev[h] = ne
                        cpe = {}
                        last_tr = None
                        for kt in range(kt0, 8):
                            b = kt % 2
                            tre = k.T(lambda t, h=h, kt=kt, b=b, pof=pof: t.transpose(
                                ps_tr[b][:], Pb[h][:, pof + kt * 128:pof + (kt + 1) * 128],
                                ident[:]),
                                waits=[ne, trr[b]], inc=True)
                            last_tr = tre
                            if kt % 2 == 0:
                                ce = k.S(lambda s, h=h, kt=kt, b=b, pof=pof: s.copy(
                                    pTt[h][:, pof + kt * 128:pof + (kt + 1) * 128],
                                    ps_tr[b][:]),
                                    waits=[tre, pt_ev[h][par] if kt < kt0 + 2 else None],
                                    inc=True, drain=True)
                            else:
                                ce = k.V(lambda v, h=h, kt=kt, b=b, pof=pof: v.tensor_copy(
                                    pTt[h][:, pof + kt * 128:pof + (kt + 1) * 128],
                                    ps_tr[b][:]),
                                    waits=[tre, pt_ev[h][par] if kt < kt0 + 2 else None],
                                    inc=True, drain=True)
                            trr[b] = ce
                            cpe[kt] = ce
                        pb_ev[h][par] = last_tr
                        last_pv = None
                        for kt in range(kt0, 8):
                            ktg = (w - 1) * 4 + kt
                            last_pv = k.T(lambda t, h=h, kt=kt, ktg=ktg, kt0=kt0, pof=pof:
                                t.matmul(
                                    ps_c[h][:], vtr[h][:, ktg * 64:(ktg + 1) * 64],
                                    pTt[h][:, pof + kt * 128:pof + (kt + 1) * 128],
                                    start=(kt == kt0), stop=(kt == 7)),
                                waits=[cpe[kt], pc_ev[h] if kt == kt0 else None],
                                inc=(kt == 7))
                        pt_ev[h][par] = last_pv
                        pc_ev[h] = k.V(lambda v, h=h, qsl=qsl: v.tensor_copy(
                            octb[h * 64:(h + 1) * 64, qsl], ps_c[h][:]),
                            waits=[last_pv], inc=True, drain=True)
            k.ld(outT[hc * 128:(hc + 1) * 128, :], octb[:],
                 waits=[pc_ev[0], pc_ev[1]], queue="s")
            k.flush("s")

        for hc in range(4):
            with contextlib.ExitStack() as ph2:
                _hc_body(hc, ph2)
            k.barrier()
    k.barrier()


def emit_glu(k, src_bf, W, b1a2d, b1g2d, dst):
    nc = k.nc
    HID = 2730
    npair = (HID + 127) // 128
    with contextlib.ExitStack() as ph:
        S = lambda n, sh, dt: ph.enter_context(nc.sbuf_tensor(_u(n), sh, dt))
        XL = [S(f"gl_x{c}", [128, T], BF16) for c in range(NCH)]
        WA = [S(f"gl_wa{i}", [128, 128 * NCH], BF16) for i in range(2)]
        WG = [S(f"gl_wg{i}", [128, 128 * NCH], BF16) for i in range(2)]
        ba = S("gl_ba", [128, npair], F32)
        bg = S("gl_bg", [128, npair], F32)
        gg = [S(f"gl_gg{i}", [128, TG], BF16) for i in range(2)]
        aa = [S(f"gl_aa{i}", [128, TG], BF16) for i in range(2)]
        stage = [S(f"gl_s{i}", [128, TG], BF16) for i in range(4)]
        pss = [ph.enter_context(nc.psum_tensor(_u("gl_ps"), [128, TG], F32)) for i in range(4)]

        for c in range(NCH):
            k.ld(XL[c][:], src_bf[c * 128:(c + 1) * 128, :], queue="s")
        k.ld(ba[:], b1a2d, queue="s")
        k.ld(bg[:], b1g2d, queue="s")
        flx = k.flush("s")
        Wr = W.rearrange("(c p) f -> p c f", p=128)
        wring = Ring(2)
        psr = Ring(2)
        psgr = Ring(2)
        ggr = Ring(2)
        str_ = Ring(4)
        ftile_flush = None
        for fp in range(npair):
            wb, wev_old = wring.nxt()
            flo = fp * 128
            fw = min(HID - flo, 128)
            k.ld(WA[wb][:, 0:NCH * fw].rearrange("p (c j) -> p c j", j=fw),
                 Wr[:, 0:NCH, flo:flo + fw], waits=[wev_old], queue="g")
            k.ld(WG[wb][:, 0:NCH * fw].rearrange("p (c j) -> p c j", j=fw),
                 Wr[:, 0:NCH, HID + flo:HID + flo + fw], waits=[wev_old], queue="g")
            wev = k.flush("g")
            f_last = None
            for g in range(NG):
                gs = slice(g * TG, (g + 1) * TG)
                pi, pev = psr.nxt()
                pgi, pgev = psgr.nxt()
                psA = pss[pi]
                psG = pss[2 + pgi]
                for c in range(NCH):
                    k.T(lambda t, psA=psA, wb=wb, c=c, fw=fw, gs=gs:
                        t.matmul(psA[0:fw, :], WA[wb][:, c * fw:(c + 1) * fw],
                                 XL[c][:, gs], start=(c == 0), stop=(c == NCH - 1)),
                        waits=[wev, pev, flx] if c == 0 else ())
                    mm = k.T(lambda t, psG=psG, wb=wb, c=c, fw=fw, gs=gs:
                             t.matmul(psG[0:fw, :], WG[wb][:, c * fw:(c + 1) * fw],
                                      XL[c][:, gs], start=(c == 0), stop=(c == NCH - 1)),
                             waits=[wev, pgev, flx] if c == 0 else (),
                             inc=(c == NCH - 1))
                f_last = mm
                gi, gev = ggr.nxt()
                ge_ = k.S(lambda s, psG=psG, gi=gi, fw=fw, fp=fp: s.activation(
                    out=gg[gi][0:fw, :], in_=psG[0:fw, :], func=AF.Gelu,
                    bias=bg[0:fw, fp:fp + 1]), waits=[mm, gev], inc=True, drain=True)
                ae_ = k.V(lambda v, psA=psA, gi=gi, fw=fw, fp=fp: v.tensor_scalar(
                    aa[gi][0:fw, :], psA[0:fw, :], ba[0:fw, fp:fp + 1], None,
                    op0=ALU.add), waits=[mm, gev], inc=True, drain=True)
                si, sev = str_.nxt()
                st = stage[si]
                he = k.V(lambda v, st=st, gi=gi, fw=fw: v.tensor_mul(
                    st[0:fw, :], aa[gi][0:fw, :], gg[gi][0:fw, :]),
                    waits=[ge_, ae_, sev, ftile_flush], inc=True, drain=True)
                ggr.done(he)
                psr.done(ae_)
                psgr.done(ge_)
                k.ld(dst[flo:flo + fw, gs], st[0:fw, :], waits=[he], queue="s")
                str_.done(None)
            ftile_flush = k.flush("s")
            wring.done(f_last)
    k.barrier()


def emit_add(k, a_f32, b_f32, out_f32):
    nc = k.nc
    with contextlib.ExitStack() as ph:
        S = lambda n, sh, dt: ph.enter_context(nc.sbuf_tensor(_u(n), sh, dt))
        ta = [S(f"ad_a{i}", [128, T], F32) for i in range(2)]
        tb = [S(f"ad_b{i}", [128, T], F32) for i in range(2)]
        chunk_flush = [None, None]
        for c in range(NCH):
            bi = c % 2
            k.ld(ta[bi][:], a_f32[c * 128:(c + 1) * 128, :], waits=[chunk_flush[bi]], queue="s")
            k.ld(tb[bi][:], b_f32[c * 128:(c + 1) * 128, :], queue="s")
            fl = k.flush("s")
            e = k.V(lambda v, bi=bi: v.tensor_add(ta[bi][:], ta[bi][:], tb[bi][:]),
                    waits=[fl], inc=True, drain=True)
            k.ld(out_f32[c * 128:(c + 1) * 128, :], ta[bi][:], waits=[e], queue="s")
            chunk_flush[bi] = k.flush("s")
    k.barrier()



def build_attn_a(k, IN, xin, xout, scratch):
    xln, qT, kT, vT, ctxT = (scratch["xln"], scratch["qT"], scratch["kT"],
                             scratch["vT"], scratch["ctxT"])
    k.load_consts(["ones_bf", "ident_bf", "ind2_bf"])
    k.barrier()
    emit_ln(k, xin, xln)
    emit_proj(k, xln, IN["L0_attn_a_WqT"], D, dst=qT, bias2d=IN["L0_attn_a_bq2d"], out_scale=0.125)
    emit_proj(k, xln, IN["L0_attn_a_WkT"], D, dst=kT, bias2d=IN["L0_attn_a_bk2d"])
    emit_proj(k, xln, IN["L0_attn_a_WvT"], D, dst=vT, bias2d=IN["L0_attn_a_bv2d"])
    emit_attn(k, qT, kT, vT, ctxT, causal=False, heads_z2d=IN["L0_attn_a_hz2d"])
    emit_proj(k, ctxT, IN["L0_attn_a_WpT"], D, bias2d=IN["L0_attn_a_bp2d"],
              hz2d=IN["L0_attn_a_hidz2d"], resid=xin, resid_out=xout)


def build_l1_attn_a(k, IN, xin, xout, scratch):
    xln, qT, kT, vT, ctxT = (scratch["xln"], scratch["qT"], scratch["kT"],
                             scratch["vT"], scratch["ctxT"])
    k.load_consts(["ones_bf", "ident_bf", "ind2_bf", "rot_il_bf"])
    k.barrier()
    emit_ln(k, xin, xln)
    emit_proj(k, xln, IN["L1_attn_a_wqT"], D, dst=qT, out_scale=0.125)
    emit_proj(k, xln, IN["L1_attn_a_wkT"], D, dst=kT)
    emit_proj(k, xln, IN["L1_attn_a_wvT"], D, dst=vT)
    emit_rope_il(k, qT)
    emit_rope_il(k, kT)
    emit_attn(k, qT, kT, vT, ctxT, causal=True)
    emit_proj(k, ctxT, IN["L1_attn_a_woT"], D, resid=xin, resid_out=xout)


def build_l1_attn_b(k, IN, xin, xout, scratch):
    xln, qT, kT, vT, ctxT = (scratch["xln"], scratch["qT"], scratch["kT"],
                             scratch["vT"], scratch["ctxT"])
    k.load_consts(["ones_bf", "ident_bf", "ind2_bf"])
    k.barrier()
    emit_ln(k, xin, xln)
    caw = IN["L1_attn_b_c_attn_w"]
    emit_proj(k, xln, caw[:, 0:1024], D, dst=qT, bias2d=IN["L1_attn_b_cabq2d"], out_scale=0.125)
    emit_proj(k, xln, caw[:, 1024:2048], D, dst=kT, bias2d=IN["L1_attn_b_cabk2d"])
    emit_proj(k, xln, caw[:, 2048:3072], D, dst=vT, bias2d=IN["L1_attn_b_cabv2d"])
    emit_attn(k, qT, kT, vT, ctxT, causal=True)
    emit_proj(k, ctxT, IN["L1_attn_b_c_proj_w"], D, bias2d=IN["L1_attn_b_cpb2d"],
              resid=xin, resid_out=xout)


def build_attn_b(k, IN, xin, xout, scratch):
    xln, qkvT = scratch["xln"], scratch["qkvT"]
    k.load_consts(["ones_bf", "indrep8_bf"])
    k.barrier()
    emit_ln(k, xin, xln)
    emit_proj(k, xln, IN["L0_attn_b_wqkv"], 3072, dst=qkvT)
    emit_attn_b(k, qkvT, xin, xout)


FFN_STAGES = [99]

def build_l0_ffn(k, IN, xin, xout, scratch):
    xln, qkvT, ctxT = scratch["xln"], scratch["qkvT"], scratch["ctxT"]
    xi, h2T = scratch["xi"], scratch["h2T"]
    k.load_consts(["ones_bf", "ident_bf", "indrep_bf", "rot_hs_bf"])
    k.barrier()
    ns = FFN_STAGES[0]
    emit_ln(k, xin, xln, dst_f32=xi)
    for l in range(2):
        emit_proj(k, xi, IN[f"L0_ffn_wqkv{l}"], 1536, dst=qkvT,
                  shift=True, src_is_f32=True)
        if ns <= l * 10 + 1: break
        emit_l2rope(k, qkvT, 0, IN[f"L0_ffn_qs2d{l}"], "xpos_qc", "xpos_qs")
        emit_l2rope(k, qkvT, 512, IN[f"L0_ffn_ks2d{l}"], "xpos_kc", "xpos_ks")
        if ns <= l * 10 + 2: break
        emit_lattn(k, qkvT, ctxT)
        if ns <= l * 10 + 3: break
        emit_proj(k, ctxT[0:512, :], IN[f"L0_ffn_wo{l}"], D, Din=512,
                  resid=xi, resid_out=xi)
        if ns <= l * 10 + 4: break
        emit_ln(k, xi, xln, gamma_beta=(IN[f"L0_ffn_lg2d{l}"], IN[f"L0_ffn_lb2d{l}"]),
                shift=True)
        emit_glu(k, xln, IN[f"L0_ffn_w1{l}"], IN[f"L0_ffn_b1a2d{l}"],
                 IN[f"L0_ffn_b1g2d{l}"], h2T)
        if ns <= l * 10 + 5: break
        emit_proj(k, h2T, IN[f"L0_ffn_w2{l}"], D, Din=2730, bias2d=IN[f"L0_ffn_b2_2d{l}"],
                  resid=xi, resid_out=xi)
    emit_add(k, xin, xi, xout)


def build_moe(k, IN, xin, xout, scratch):
    xln = scratch["xln"]
    k.load_consts(["ones_bf", "ident_f32"])
    k.barrier()
    emit_ln(k, xin, xln)
    emit_moe(k, xln, IN["L1_ffn_rwT"], IN["L1_ffn_expert_weights"], xin, xout)



def _prep_weights(inputs):
    P = {}
    f32 = np.float32
    def t(x):
        return np.ascontiguousarray(np.asarray(x, f32).T)
    def vec2d(x):
        return np.ascontiguousarray(np.asarray(x, f32).reshape(-1, 128).T)
    P["L0_attn_a_WqT"] = t(inputs["L0_attn_a_Wq"])
    P["L0_attn_a_WkT"] = t(inputs["L0_attn_a_Wk"])
    P["L0_attn_a_WvT"] = t(inputs["L0_attn_a_Wv"])
    P["L0_attn_a_WpT"] = t(inputs["L0_attn_a_Wp"])
    P["L0_attn_a_bq2d"] = vec2d(inputs["L0_attn_a_bq"])
    P["L0_attn_a_bk2d"] = vec2d(inputs["L0_attn_a_bk"])
    P["L0_attn_a_bv2d"] = vec2d(inputs["L0_attn_a_bv"])
    P["L0_attn_a_bp2d"] = vec2d(inputs["L0_attn_a_bp"])
    P["L0_attn_a_hz2d"] = np.ascontiguousarray(
        np.asarray(inputs["L0_attn_a_heads_z"], f32).reshape(1, 16))
    P["L0_attn_a_hidz2d"] = vec2d(inputs["L0_attn_a_hidden_z"])
    P["L1_attn_a_wqT"] = t(inputs["L1_attn_a_wq"])
    P["L1_attn_a_wkT"] = t(inputs["L1_attn_a_wk"])
    P["L1_attn_a_wvT"] = t(inputs["L1_attn_a_wv"])
    P["L1_attn_a_woT"] = t(inputs["L1_attn_a_wo"])
    P["L1_attn_b_c_attn_w"] = np.ascontiguousarray(np.asarray(inputs["L1_attn_b_c_attn_w"], f32))
    P["L1_attn_b_c_proj_w"] = np.ascontiguousarray(np.asarray(inputs["L1_attn_b_c_proj_w"], f32))
    cab = np.asarray(inputs["L1_attn_b_c_attn_b"], f32)
    P["L1_attn_b_cabq2d"] = vec2d(cab[0:1024])
    P["L1_attn_b_cabk2d"] = vec2d(cab[1024:2048])
    P["L1_attn_b_cabv2d"] = vec2d(cab[2048:3072])
    P["L1_attn_b_cpb2d"] = vec2d(inputs["L1_attn_b_c_proj_b"])
    P["L1_ffn_rwT"] = t(inputs["L1_ffn_router_w"])
    P["L1_ffn_expert_weights"] = np.ascontiguousarray(np.asarray(inputs["L1_ffn_expert_weights"], f32))
    wqkv = np.asarray(inputs["L0_attn_b_wqkv"], f32)
    perm = np.concatenate([np.arange(16) * 192 + s * 64 + np.arange(64)[:, None]
                           for s in range(3)], axis=1).T.reshape(-1)
    perm = np.zeros(3072, np.int64)
    for s in range(3):
        for h in range(16):
            for d_ in range(64):
                perm[s * 1024 + h * 64 + d_] = 192 * h + 64 * s + d_
    P["L0_attn_b_wqkv"] = np.ascontiguousarray(wqkv[:, perm])
    HID = 2730
    for l in range(2):
        P[f"L0_ffn_wqkv{l}"] = np.ascontiguousarray(np.asarray(inputs["L0_ffn_wqkv"], f32)[l])
        P[f"L0_ffn_wo{l}"] = np.ascontiguousarray(np.asarray(inputs["L0_ffn_wo"], f32)[l])
        P[f"L0_ffn_w1{l}"] = np.ascontiguousarray(np.asarray(inputs["L0_ffn_w1"], f32)[l])
        P[f"L0_ffn_w2{l}"] = np.ascontiguousarray(np.asarray(inputs["L0_ffn_w2"], f32)[l])
        qs = np.asarray(inputs["L0_ffn_q_scale"], f32)[l]
        ks = np.asarray(inputs["L0_ffn_k_scale"], f32)[l]
        P[f"L0_ffn_qs2d{l}"] = np.ascontiguousarray(np.concatenate([qs, qs]).reshape(128, 1))
        P[f"L0_ffn_ks2d{l}"] = np.ascontiguousarray(np.concatenate([ks, ks]).reshape(128, 1))
        P[f"L0_ffn_lg2d{l}"] = vec2d(np.asarray(inputs["L0_ffn_ln_g"], f32)[l])
        P[f"L0_ffn_lb2d{l}"] = vec2d(np.asarray(inputs["L0_ffn_ln_b"], f32)[l])
        b1 = np.asarray(inputs["L0_ffn_b1"], f32)[l]
        npair = 22
        b1a = np.zeros((128, npair), f32)
        b1g = np.zeros((128, npair), f32)
        for j in range(npair):
            n = min(HID - j * 128, 128)
            b1a[0:n, j] = b1[j * 128:j * 128 + n]
            b1g[0:n, j] = b1[HID + j * 128:HID + j * 128 + n]
        P[f"L0_ffn_b1a2d{l}"] = b1a
        P[f"L0_ffn_b1g2d{l}"] = b1g
        P[f"L0_ffn_b2_2d{l}"] = vec2d(np.asarray(inputs["L0_ffn_b2"], f32)[l])
    return P

FULL_ORDER = ["attn_a", "l0_ffn", "attn_b", "l0_ffn", "l1_attn_a", "moe", "l1_attn_b", "moe"]
SUBLAYERS = list(FULL_ORDER)
SUB_W = {
    "attn_a": ["L0_attn_a_WqT", "L0_attn_a_WkT", "L0_attn_a_WvT", "L0_attn_a_WpT",
               "L0_attn_a_bq2d", "L0_attn_a_bk2d", "L0_attn_a_bv2d", "L0_attn_a_bp2d",
               "L0_attn_a_hz2d", "L0_attn_a_hidz2d"],
    "l1_attn_a": ["L1_attn_a_wqT", "L1_attn_a_wkT", "L1_attn_a_wvT", "L1_attn_a_woT"],
    "l1_attn_b": ["L1_attn_b_c_attn_w", "L1_attn_b_c_proj_w", "L1_attn_b_cabq2d",
                  "L1_attn_b_cabk2d", "L1_attn_b_cabv2d", "L1_attn_b_cpb2d"],
    "moe": ["L1_ffn_rwT", "L1_ffn_expert_weights"],
    "attn_b": ["L0_attn_b_wqkv"],
    "l0_ffn": (["L0_ffn_wqkv0", "L0_ffn_wo0", "L0_ffn_w10", "L0_ffn_w20",
                "L0_ffn_qs2d0", "L0_ffn_ks2d0", "L0_ffn_lg2d0", "L0_ffn_lb2d0",
                "L0_ffn_b1a2d0", "L0_ffn_b1g2d0", "L0_ffn_b2_2d0"] +
               ["L0_ffn_wqkv1", "L0_ffn_wo1", "L0_ffn_w11", "L0_ffn_w21",
                "L0_ffn_qs2d1", "L0_ffn_ks2d1", "L0_ffn_lg2d1", "L0_ffn_lb2d1",
                "L0_ffn_b1a2d1", "L0_ffn_b1g2d1", "L0_ffn_b2_2d1"]),
}

def _build_nc(sub, wshapes):
    nc = bass.Bass("TRN2", target_bir_lowering=False, debug=False)
    es = contextlib.ExitStack()
    nc._es = es
    k = Ktx(nc, es)
    IN = {}
    for name, shp in wshapes.items():
        IN[name] = nc.dram_tensor(name, list(shp), F32, kind="ExternalInput").ap()
    xin = nc.dram_tensor("xT", [D, T], F32, kind="ExternalInput").ap()
    xout = nc.dram_tensor("xT_out", [D, T], F32, kind="ExternalOutput").ap()
    if sub == "moe" and _os.environ.get("MOE_DBG"):
        MOE_DBG[0] = nc.dram_tensor("moe_dbg", [128, 4096], F32, kind="ExternalOutput").ap()
    scratch = {
        "xln": nc.dram_tensor("s_xln", [D, T], BF16).ap(),
        "qT": nc.dram_tensor("s_qT", [D, T], BF16).ap(),
        "kT": nc.dram_tensor("s_kT", [D, T], BF16).ap(),
        "vT": nc.dram_tensor("s_vT", [D, T], BF16).ap(),
        "ctxT": nc.dram_tensor("s_ctxT", [D, T], BF16).ap(),
        "qkvT": nc.dram_tensor("s_qkvT", [3072, T], BF16).ap(),
        "xi": nc.dram_tensor("s_xi", [D, T], F32).ap(),
        "h2T": nc.dram_tensor("s_h2T", [2730, T], BF16).ap(),
    }
    k.drows = nc.dram_tensor("s_rows", [16, T], F32).ap()
    es.__enter__()
    builders = {"attn_a": build_attn_a, "l1_attn_a": build_l1_attn_a,
                "l1_attn_b": build_l1_attn_b, "moe": build_moe,
                "attn_b": build_attn_b, "l0_ffn": build_l0_ffn}
    builders[sub](k, IN, xin, xout, scratch)
    k.barrier()
    k.finalize()
    es.__exit__(None, None, None)
    return nc, list(k.dram_consts.keys())


def run_sublayer(sub, P, xTs):
    wnames = {n: P[n].shape for n in SUB_W[sub]}
    nc, cnames = _build_nc(sub, wnames)
    base = {n: P[n] for n in wnames}
    for cn in cnames:
        base["const_" + cn] = np.ascontiguousarray(CONSTS[cn])
    in_maps = [dict(base, xT=np.ascontiguousarray(x)) for x in xTs]
    res = run_bass_kernel_spmd(nc, in_maps, core_ids=list(range(8)))
    if MOE_DBG[0] is not None:
        np.save("_moe_dbg.npy", np.asarray(res.results[0]["moe_dbg"], np.float32))
    return [np.asarray(r["xT_out"], np.float32) for r in res.results]


def _build_full_nc(wshapes):
    nc = bass.Bass("TRN2", target_bir_lowering=False, debug=False)
    es = contextlib.ExitStack()
    nc._es = es
    k = Ktx(nc, es)
    IN = {}
    for name, shp in wshapes.items():
        IN[name] = nc.dram_tensor(name, list(shp), F32, kind="ExternalInput").ap()
    xext = nc.dram_tensor("xT", [D, T], F32, kind="ExternalInput").ap()
    xout_ext = nc.dram_tensor("xT_out", [D, T], F32, kind="ExternalOutput").ap()
    xA = nc.dram_tensor("s_xA", [D, T], F32).ap()
    xB = nc.dram_tensor("s_xB", [D, T], F32).ap()
    scratch = {
        "xln": nc.dram_tensor("s_xln", [D, T], BF16).ap(),
        "qT": nc.dram_tensor("s_qT", [D, T], BF16).ap(),
        "kT": nc.dram_tensor("s_kT", [D, T], BF16).ap(),
        "vT": nc.dram_tensor("s_vT", [D, T], BF16).ap(),
        "ctxT": nc.dram_tensor("s_ctxT", [D, T], BF16).ap(),
        "qkvT": nc.dram_tensor("s_qkvT", [3072, T], BF16).ap(),
        "xi": nc.dram_tensor("s_xi", [D, T], F32).ap(),
        "h2T": nc.dram_tensor("s_h2T", [2730, T], BF16).ap(),
    }
    k.drows = nc.dram_tensor("s_rows", [16, T], F32).ap()
    es.__enter__()
    builders = {"attn_a": build_attn_a, "l1_attn_a": build_l1_attn_a,
                "l1_attn_b": build_l1_attn_b, "moe": build_moe,
                "attn_b": build_attn_b, "l0_ffn": build_l0_ffn}
    cur = xext
    nxt = xA
    for i, sub in enumerate(FULL_ORDER):
        out = xout_ext if i == len(FULL_ORDER) - 1 else nxt
        builders[sub](k, IN, cur, out, scratch)
        cur, nxt = out, (xB if out is xA else xA)
    k.barrier()
    k.finalize()
    es.__exit__(None, None, None)
    return nc, list(k.dram_consts.keys())


def kernel(**inputs):
    P = _prep_weights(inputs)
    x = np.asarray(inputs["L0_attn_a_inputs"], np.float32)
    xTs = [np.ascontiguousarray(x[b].T) for b in range(8)]
    if _os.environ.get("KMULTI"):
        for sub in SUBLAYERS:
            xTs = run_sublayer(sub, P, xTs)
    else:
        wnames = {}
        for sub in set(FULL_ORDER):
            for n in SUB_W[sub]:
                wnames[n] = P[n].shape
        nc, cnames = _build_full_nc(wnames)
        base = {n: P[n] for n in wnames}
        for cn in cnames:
            base["const_" + cn] = np.ascontiguousarray(CONSTS[cn])
        in_maps = [dict(base, xT=np.ascontiguousarray(xt)) for xt in xTs]
        res = run_bass_kernel_spmd(nc, in_maps, core_ids=list(range(8)))
        xTs = [np.asarray(r["xT_out"], np.float32) for r in res.results]
    out = np.stack([xt.T for xt in xTs], 0)
    return out.astype(np.float32)


# version 07
# speedup vs baseline: 1.0784x; 1.0784x over previous
import contextlib
import os as _os
import numpy as np
import ml_dtypes
import concourse.bass as bass
import concourse.mybir as mybir
from concourse.bass_isa import ReduceOp
from concourse.bass_utils import run_bass_kernel_spmd

F32 = mybir.dt.float32
BF16 = mybir.dt.bfloat16
AF = mybir.ActivationFunctionType
ALU = mybir.AluOpType
AX = mybir.AxisListType.X
BF = ml_dtypes.bfloat16

D = 1024
T = 4096
NCH = 8
NG = 8
TG = 512
NEG = -1.0e30


def _build_constants():
    c = {}
    c["ones_bf"] = np.ones((128, 128), BF)
    c["ident_bf"] = np.eye(128, dtype=np.float32).astype(BF)
    c["ident_f32"] = np.eye(128, dtype=np.float32)
    ind2 = np.zeros((128, 2), np.float32)
    ind2[0:64, 0] = 1.0
    ind2[64:128, 1] = 1.0
    c["ind2_bf"] = ind2.astype(BF)
    ind = np.zeros((128, 128), np.float32)
    ind[0:64, 0:64] = 1.0
    ind[64:128, 64:128] = 1.0
    c["indrep_bf"] = ind.astype(BF)
    c["indrep8_bf"] = (ind * 0.125).astype(BF)
    pi = np.zeros((128, 128), np.float32)
    for blk in (0, 64):
        for i in range(32):
            pi[blk + 2 * i + 1, blk + 2 * i] = -1.0
            pi[blk + 2 * i, blk + 2 * i + 1] = 1.0
    c["rot_il_bf"] = pi.astype(BF)
    ph = np.zeros((128, 128), np.float32)
    for blk in (0, 64):
        for l in range(32):
            ph[blk + l + 32, blk + l] = -1.0
            ph[blk + l, blk + l + 32] = 1.0
    c["rot_hs_bf"] = ph.astype(BF)
    dh = 64
    t = np.arange(T, dtype=np.float64)
    inv = 10000.0 ** (-np.arange(0, dh, 2, dtype=np.float64) / dh)
    ang = np.outer(inv, t)
    cos2 = np.zeros((64, T)); sin2 = np.zeros((64, T))
    cos2[0::2] = np.cos(ang); cos2[1::2] = np.cos(ang)
    sin2[0::2] = np.sin(ang); sin2[1::2] = np.sin(ang)
    c["ropeA_cos"] = np.concatenate([cos2, cos2], 0).astype(np.float32)
    c["ropeA_sin"] = np.concatenate([sin2, sin2], 0).astype(np.float32)
    WINDOW = 512
    half_idx = np.arange(0, dh, 2, dtype=np.float64)
    inv_freq = 1.0 / 10000.0 ** (half_idx / dh)
    freqs = np.outer(inv_freq, t)
    base = (half_idx + 0.4 * dh) / (1.4 * dh)
    power = (t - T // 2) / float(WINDOW // 2)
    scale = base[:, None] ** power[None, :]
    cf, sf = np.cos(freqs), np.sin(freqs)
    qc = np.concatenate([cf * scale, cf * scale], 0) * 8.0
    qs = np.concatenate([sf * scale, sf * scale], 0) * 8.0
    kc = np.concatenate([cf / scale, cf / scale], 0)
    ks = np.concatenate([sf / scale, sf / scale], 0)
    c["xpos_qc"] = np.concatenate([qc, qc], 0).astype(np.float32)
    c["xpos_qs"] = np.concatenate([qs, qs], 0).astype(np.float32)
    c["xpos_kc"] = np.concatenate([kc, kc], 0).astype(np.float32)
    c["xpos_ks"] = np.concatenate([ks, ks], 0).astype(np.float32)
    i = np.arange(512)[:, None]
    j = np.arange(1024)[None, :]
    m = np.where((j >= i) & (j <= i + 512), 0.0, NEG).astype(np.float32)
    c["lmask"] = m
    mT = np.full((128, 8 * 512), NEG, np.float32)
    pp = np.arange(128)[:, None]
    cq = np.arange(512)[None, :]
    for kt in range(8):
        allow = (cq <= kt * 128 + pp) & (cq >= kt * 128 + pp - 512)
        mT[:, kt * 512:(kt + 1) * 512] = np.where(allow, 0.0, NEG)
    c["lmaskT"] = mT
    return c

CONSTS = _build_constants()


ENGINES = ("tensor", "vector", "scalar", "gpsimd", "sync")
_OWN = {"tD": "tensor", "vD": "vector", "sD": "scalar", "gD": "gpsimd",
        "fS": "sync", "fG": "gpsimd"}

_UNIQ = [0]
def _u(name):
    _UNIQ[0] += 1
    return f"{name}_{_UNIQ[0]}"

class Ktx:
    def __init__(self, nc, es):
        self.nc = nc
        self.es = es
        self.ops = []
        self.sems = {}
        self.cnt = {}
        self.consts_sb = {}
        self.dram_consts = {}
        self.drows = None
        self.cpool = es.enter_context(nc.sbuf_tensor("cpool", [128, 16], F32))
        self.cmap = {}

    def semh(self, name):
        if name not in self.sems:
            self.sems[name] = self.es.enter_context(self.nc.semaphore(name))
            self.cnt[name] = 0
        return self.sems[name]

    def _op(self, eng, fn, waits=(), inc=None, inc_by=1, drain=False):
        ev = None
        if inc is not None:
            self.semh(inc)
            self.cnt[inc] += inc_by
            ev = (inc, self.cnt[inc])
        ws = [w for w in waits if w is not None]
        for w in ws:
            self.semh(w[0])
        self.ops.append((eng, fn, ws, (inc, inc_by) if inc else None, drain))
        return ev

    def T(self, fn, waits=(), inc=False):
        return self._op("tensor", fn, waits, "tD" if inc else None)
    def V(self, fn, waits=(), inc=False, drain=False):
        return self._op("vector", fn, waits, "vD" if inc else None, drain=drain)
    def S(self, fn, waits=(), inc=False, drain=False):
        return self._op("scalar", fn, waits, "sD" if inc else None, drain=drain)
    def G(self, fn, waits=(), inc=False, drain=False):
        return self._op("gpsimd", fn, waits, "gD" if inc else None, drain=drain)

    def ld(self, out, in_, waits=(), queue="s"):
        eng, sem = ("sync", "qS") if queue == "s" else ("gpsimd", "qG")
        fn = lambda e, out=out, in_=in_: e.dma_start(out=out, in_=in_)
        self._op(eng, fn, waits, sem, inc_by=16)

    def dmat(self, out, in_, waits=()):
        fn = lambda e, out=out, in_=in_: e.dma_start_transpose(out, in_)
        self._op("sync", fn, waits, "qS", inc_by=16)

    def flush(self, queue="s"):
        eng, qn, fname = (("sync", "qS", "fS") if queue == "s"
                          else ("gpsimd", "qG", "fG"))
        self.semh(qn)
        tot = self.cnt[qn]
        def fn(e, tot=tot, qn=qn):
            if tot:
                e.wait_ge(self.sems[qn], tot)
            return e.nop()
        return self._op(eng, fn, (), fname)

    def barrier(self):
        self.semh("bar")
        qs_tot = self.cnt.get("qS", 0)
        qg_tot = self.cnt.get("qG", 0)
        target = self.cnt["bar"] + len(ENGINES)
        for e in ENGINES:
            def fn(eng, qs=qs_tot, qg=qg_tot, tgt=target):
                if qs:
                    eng.wait_ge(self.sems["qS"], qs)
                if qg:
                    eng.wait_ge(self.sems["qG"], qg)
                eng.sem_inc(self.sems["bar"], 1)
                eng.wait_ge(self.sems["bar"], tgt)
                return None
            self._op(e, fn)
        self.cnt["bar"] = target

    def cscalar(self, value):
        if value not in self.cmap:
            col = len(self.cmap)
            self.cmap[value] = col
            self.V(lambda v, col=col, value=value: v.memset(self.cpool[:, col:col + 1], value),
                   inc=True, drain=True)
        return self.cpool[:, self.cmap[value]:self.cmap[value] + 1]

    def load_consts(self, names):
        for key in names:
            if key in self.consts_sb:
                continue
            arr = CONSTS[key]
            dt = BF16 if arr.dtype == BF else F32
            h = self.es.enter_context(self.nc.sbuf_tensor("c_" + key, list(arr.shape), dt))
            self.consts_sb[key] = h
            self.ld(h[:], self.dconst(key), queue="s")

    def dconst(self, key):
        if key not in self.dram_consts:
            arr = CONSTS[key]
            dt = BF16 if arr.dtype == BF else F32
            self.dram_consts[key] = self.nc.dram_tensor(
                "const_" + key, list(arr.shape), dt, kind="ExternalInput").ap()
        return self.dram_consts[key]

    def finalize(self):
        blk = self.es.enter_context(self.nc.Block())
        for ename in ENGINES:
            def mk(ename):
                def emit(eng):
                    waited = {}
                    for e, fn, ws, inc, drain in self.ops:
                        if e != ename:
                            continue
                        for (sn, val) in ws:
                            if _OWN.get(sn) == ename:
                                continue
                            if waited.get(sn, -1) >= val:
                                continue
                            eng.wait_ge(self.sems[sn], val)
                            waited[sn] = val
                        inst = fn(eng)
                        if inc is not None:
                            sn, by = inc
                            inst.then_inc(self.sems[sn], by)
                        if drain:
                            eng.drain()
                return emit
            getattr(blk, ename)(mk(ename))


class Ring:
    def __init__(self, n):
        self.ev = [None] * n
        self.n = n
        self.i = -1
    def nxt(self):
        self.i = (self.i + 1) % self.n
        return self.i, self.ev[self.i]
    def done(self, ev):
        self.ev[self.i] = ev



def emit_ln(k, src, dst_bf, dst_f32=None, gamma_beta=None, shift=False, eps=1e-5):
    nc = k.nc
    with contextlib.ExitStack() as ph:
        S = lambda n, sh, dt: ph.enter_context(nc.sbuf_tensor(_u(n), sh, dt))
        xb = [S(f"ln_xb{c}", [128, T], BF16) for c in range(NCH)]
        xq = [S(f"ln_xq{i}", [128, TG], BF16) for i in range(2)]
        rows = S("ln_rows", [1, TG * 4], F32)
        Rb = S("ln_R", [128, TG], F32)
        Mb = S("ln_M", [128, TG], F32)
        gtile = S("ln_g", [128, NCH], F32) if gamma_beta else None
        btile = S("ln_b", [128, NCH], F32) if gamma_beta else None
        ot = [S(f"ln_o{i}", [128, TG], BF16) for i in range(NCH)]
        o32 = [S(f"ln_o32_{i}", [128, TG], F32) for i in range(NCH)] if dst_f32 is not None else None
        ps_s = ph.enter_context(nc.psum_tensor(_u("ln_ps_s"), [1, TG], F32))
        ps_q = ph.enter_context(nc.psum_tensor(_u("ln_ps_q"), [1, TG], F32))
        ones1 = k.consts_sb["ones_bf"]
        drow = k.drows

        if gamma_beta:
            g_dram, b_dram = gamma_beta
            k.ld(gtile[:], g_dram, queue="s")
            k.ld(btile[:], b_dram, queue="s")
        mz = []
        for c in range(NCH):
            if shift and c >= 4:
                k.ld(xb[c][:, 1:T], src[c * 128:(c + 1) * 128, 0:T - 1], queue="g")
                mz.append(k.V(lambda v, c=c: v.memset(xb[c][:, 0:1], 0.0), inc=True))
            else:
                k.ld(xb[c][:], src[c * 128:(c + 1) * 128, :], queue="g")
        lev = [k.flush("g"), k.flush("s")] + mz

        xqr = Ring(2)
        group_flush = None
        stats_read = None
        for g in range(NG):
            gs = slice(g * TG, (g + 1) * TG)
            for c in range(NCH):
                k.T(lambda t, c=c, gs=gs: t.matmul(
                    ps_s[:], ones1[:, 0:1], xb[c][:, gs],
                    start=(c == 0), stop=(c == NCH - 1)),
                    waits=lev + ([stats_read] if c == 0 else []))
            for c in range(NCH):
                qi, qev = xqr.nxt()
                sq = k.S(lambda s, c=c, qi=qi, gs=gs: s.activation(
                    out=xq[qi][:], in_=xb[c][:, gs], func=AF.Square),
                    waits=lev + [qev], inc=True, drain=True)
                mm = k.T(lambda t, qi=qi, c=c: t.matmul(
                    ps_q[:], ones1[:, 0:1], xq[qi][:],
                    start=(c == 0), stop=(c == NCH - 1)), waits=[sq], inc=True)
                xqr.done(mm)
            mu = rows[:, 0:TG]; var = rows[:, TG:2 * TG]
            rstd = rows[:, 2 * TG:3 * TG]; mur = rows[:, 3 * TG:4 * TG]
            k.V(lambda v: v.tensor_scalar(mu, ps_s[:], 1.0 / D, None, op0=ALU.mult),
                waits=[("tD", k.cnt["tD"])], drain=True)
            k.V(lambda v: v.tensor_scalar(var, ps_q[:], 1.0 / D, None, op0=ALU.mult),
                drain=True)
            k.V(lambda v: v.tensor_mul(mur, mu, mu), drain=True)
            e3 = k.V(lambda v: v.tensor_sub(var, var, mur), inc=True, drain=True)
            stats_read = e3
            epsap = k.cscalar(float(eps))
            k.S(lambda s, epsap=epsap: s.activation(out=rstd, in_=var, func=AF.Ln,
                                                    bias=epsap[0:1, :]),
                waits=[e3, ("vD", k.cnt["vD"])], drain=True)
            e5 = k.S(lambda s: s.activation(out=rstd, in_=rstd, func=AF.Exp, scale=-0.5),
                     inc=True, drain=True)
            e6 = k.V(lambda v: v.tensor_mul(mur, mu, rstd), waits=[e5], inc=True, drain=True)
            k.ld(drow[0:1, 0:TG], rstd, waits=[e6], queue="s")
            k.ld(drow[1:2, 0:TG], mur, waits=[e6], queue="s")
            f1 = k.flush("s")
            k.ld(Rb[:], drow[0:1, 0:TG].partition_broadcast(128), waits=[f1], queue="s")
            k.ld(Mb[:], drow[1:2, 0:TG].partition_broadcast(128), waits=[f1], queue="s")
            f2 = k.flush("s")
            for c in range(NCH):
                o = ot[c]
                k.V(lambda v, c=c, o=o, gs=gs: v.tensor_mul(o[:], xb[c][:, gs], Rb[:]),
                    waits=[f2, group_flush], drain=True)
                last = k.V(lambda v, o=o: v.tensor_sub(o[:], o[:], Mb[:]),
                           inc=True, drain=True)
                if gamma_beta:
                    last = k.V(lambda v, c=c, o=o: v.tensor_scalar(
                        o[:], o[:], gtile[:, c:c + 1], btile[:, c:c + 1],
                        op0=ALU.mult, op1=ALU.add), inc=True, drain=True)
                if dst_f32 is not None:
                    o3 = o32[c]
                    c32 = k.V(lambda v, o=o, o3=o3: v.tensor_copy(o3[:], o[:]),
                              inc=True, drain=True)
                    k.ld(dst_f32[c * 128:(c + 1) * 128, gs], o3[:], waits=[c32], queue="s")
                k.ld(dst_bf[c * 128:(c + 1) * 128, gs], o[:], waits=[last], queue="s")
            group_flush = k.flush("s")
    k.barrier()


def _evac_plain(k, ps, st, fw, bias_ap, out_scale, waits):
    if bias_ap is not None and out_scale is not None:
        return k.V(lambda v: v.tensor_scalar(st[0:fw, :], ps[0:fw, :], bias_ap,
                                             out_scale, op0=ALU.add, op1=ALU.mult),
                   waits=waits, inc=True, drain=True)
    if bias_ap is not None:
        return k.V(lambda v: v.tensor_scalar(st[0:fw, :], ps[0:fw, :], bias_ap,
                                             None, op0=ALU.add),
                   waits=waits, inc=True, drain=True)
    if out_scale is not None:
        return k.V(lambda v: v.tensor_scalar(st[0:fw, :], ps[0:fw, :], out_scale,
                                             None, op0=ALU.mult),
                   waits=waits, inc=True, drain=True)
    return k.S(lambda s: s.copy(st[0:fw, :], ps[0:fw, :]),
               waits=waits, inc=True, drain=True)


def emit_proj(k, src_bf, W, F_out, dst=None, bias2d=None, out_scale=None,
              resid=None, resid_out=None, hz2d=None, Din=D, shift=False,
              src_is_f32=False):
    nkc = (Din + 127) // 128
    nsplit = 1
    while nkc * (T // nsplit) * 2 * 128 > 12 * 2 ** 20:
        nsplit *= 2
    for sp in range(nsplit):
        _emit_proj_span(k, src_bf, W, F_out, dst, bias2d, out_scale, resid,
                        resid_out, hz2d, Din, shift, src_is_f32,
                        sp * (T // nsplit), (sp + 1) * (T // nsplit))


def _emit_proj_span(k, src_bf, W, F_out, dst, bias2d, out_scale, resid,
                    resid_out, hz2d, Din, shift, src_is_f32, t0, t1):
    nc = k.nc
    TS = t1 - t0
    ng = TS // TG
    nkc = (Din + 127) // 128
    nf = (F_out + 127) // 128
    with contextlib.ExitStack() as ph:
        S = lambda n, sh, dt: ph.enter_context(nc.sbuf_tensor(_u(n), sh, dt))
        XL = [S(f"pj_x{c}", [128, TS], BF16) for c in range(nkc)]
        WT = [S(f"pj_w{i}", [128, 128 * nkc], BF16) for i in range(2)]
        btile = S("pj_b", [128, nf], F32) if bias2d is not None else None
        hztile = S("pj_hz", [128, nf], F32) if hz2d is not None else None
        stage = [S(f"pj_s{i}", [128, TG], BF16) for i in range(4)]
        rstage = [S(f"pj_r{i}", [128, TG], F32) for i in range(4)]
        xres = [S(f"pj_xr{i}", [128, TG], F32) for i in range(ng)] if resid is not None else None
        pss = [ph.enter_context(nc.psum_tensor(_u("pj_ps"), [128, TG], F32)) for i in range(4)]

        q_ = "g" if src_is_f32 else "s"
        mz = []
        for c in range(nkc):
            rlo, rhi = c * 128, min(Din, (c + 1) * 128)
            if shift and c >= nkc // 2:
                if t0 == 0:
                    k.ld(XL[c][0:rhi - rlo, 1:TS], src_bf[rlo:rhi, 0:TS - 1], queue=q_)
                    mz.append(k.V(lambda v, c=c: v.memset(XL[c][:, 0:1], 0.0), inc=True))
                else:
                    k.ld(XL[c][0:rhi - rlo, :], src_bf[rlo:rhi, t0 - 1:t1 - 1], queue=q_)
            else:
                k.ld(XL[c][0:rhi - rlo, :], src_bf[rlo:rhi, t0:t1], queue=q_)
        if bias2d is not None:
            k.ld(btile[:], bias2d, queue="s")
        if hz2d is not None:
            k.ld(hztile[:], hz2d, queue="s")
        flx = [k.flush("s"), k.flush("g")] + mz

        wring = Ring(2)
        psr = Ring(4)
        str_ = Ring(4)
        Wr = W.rearrange("(c p) f -> p c f", p=128) if Din % 128 == 0 else None
        ftile_flush = None
        for f in range(nf):
            wb, wev_old = wring.nxt()
            flo, fhi = f * 128, min(F_out, (f + 1) * 128)
            fw = fhi - flo
            if Wr is not None:
                k.ld(WT[wb][:, 0:nkc * fw].rearrange("p (c j) -> p c j", j=fw),
                     Wr[:, 0:nkc, flo:fhi], waits=[wev_old], queue="g")
            else:
                for c in range(nkc):
                    kw = min(Din, (c + 1) * 128) - c * 128
                    k.ld(WT[wb][0:kw, c * fw:(c + 1) * fw],
                         W[c * 128:c * 128 + kw, flo:fhi],
                         waits=[wev_old] if c == 0 else (), queue="g")
            wev = k.flush("g")
            xflush = None
            if resid is not None:
                for g in range(ng):
                    k.ld(xres[g][0:fw, :], resid[flo:fhi, t0 + g * TG:t0 + (g + 1) * TG],
                         waits=[ftile_flush], queue="s")
                xflush = k.flush("s")
            f_last_mm = None
            for g in range(ng):
                gs = slice(g * TG, (g + 1) * TG)
                gabs = slice(t0 + g * TG, t0 + (g + 1) * TG)
                pi, pev = psr.nxt()
                ps = pss[pi]
                for c in range(nkc):
                    kw = min(Din, (c + 1) * 128) - c * 128
                    mm = k.T(lambda t, ps=ps, wb=wb, c=c, fw=fw, gs=gs, kw=kw:
                             t.matmul(ps[0:fw, :], WT[wb][0:kw, c * fw:(c + 1) * fw],
                                      XL[c][0:kw, gs], start=(c == 0), stop=(c == nkc - 1)),
                             waits=([wev, pev] + flx) if c == 0 else (),
                             inc=(c == nkc - 1))
                f_last_mm = mm
                si, sev_old = str_.nxt()
                if resid is None:
                    st = stage[si]
                    bap = btile[0:fw, f:f + 1] if bias2d is not None else None
                    e = _evac_plain(k, ps, st, fw, bap, out_scale,
                                    [mm, sev_old, ftile_flush])
                    k.ld(dst[flo:fhi, gabs], st[0:fw, :], waits=[e], queue="s")
                    str_.done(e)
                    psr.done(e)
                else:
                    rt = rstage[si]
                    xr = xres[g]
                    cur = ps
                    if bias2d is not None:
                        k.V(lambda v, rt=rt, ps=ps, f=f, fw=fw:
                            v.tensor_scalar(rt[0:fw, :], ps[0:fw, :],
                                            btile[0:fw, f:f + 1], None, op0=ALU.add),
                            waits=[mm, sev_old, ftile_flush], drain=True)
                        cur = rt
                    if hz2d is not None:
                        k.V(lambda v, rt=rt, cur=cur, f=f, fw=fw:
                            v.tensor_scalar(rt[0:fw, :], cur[0:fw, :],
                                            hztile[0:fw, f:f + 1], None, op0=ALU.mult),
                            waits=[mm, sev_old, ftile_flush], drain=True)
                        cur = rt
                    e = k.V(lambda v, rt=rt, cur=cur, xr=xr, fw=fw:
                            v.tensor_add(rt[0:fw, :], cur[0:fw, :], xr[0:fw, :]),
                            waits=[xflush, mm, sev_old, ftile_flush], inc=True, drain=True)
                    k.ld(resid_out[flo:fhi, gabs], rt[0:fw, :], waits=[e], queue="s")
                    str_.done(e)
                    psr.done(e)
            ftile_flush = k.flush("s")
            wring.done(f_last_mm)
    k.barrier()


def emit_attn(k, qT, kT, vT, ctxT, causal, heads_z2d=None, nheads=16):
    nc = k.nc
    ident = k.consts_sb["ident_bf"]
    ind2 = k.consts_sb["ind2_bf"]
    drow = k.drows
    NT = T // 128

    def _hc_body(hc, ph):
        S = lambda n, sh, dt: ph.enter_context(nc.sbuf_tensor(_u(n), sh, dt))
        if True:
            q2 = S("at_q2", [128, T], BF16)
            k2 = S("at_k2", [128, T], BF16)
            v2 = S("at_v2", [128, T], BF16)
            vtr = [S(f"at_vtr{h}", [128, NT * 65], BF16) for h in range(2)]
            sq = S("at_sq", [128, T], BF16)
            nrm = S("at_nrm", [2, NG], F32)
            brow = S("at_brow", [2, 4], F32)
            negb = [S(f"at_negb{h}", [128, 1], F32) for h in range(2)]
            hzb = [S(f"at_hz{h}", [64, 1], F32) for h in range(2)] if heads_z2d is not None else None
            pT = [S(f"at_pT{h}", [128, NT * TG], BF16) for h in range(2)]
            recr = S("at_rec", [128, TG], F32)
            recb = [S(f"at_recb{h}", [64, TG], F32) for h in range(2)]
            sctx = [S(f"at_sctx{h}", [64, TG], BF16) for h in range(2)]
            octx = [S(f"at_octx{h}", [64, TG], BF16) for h in range(2)]
            ps_tr = ph.enter_context(nc.psum_tensor(_u("at_ps_tr"), [128, 64], BF16))
            ps_n = ph.enter_context(nc.psum_tensor(_u("at_ps_n"), [2, TG], F32))
            ps_s = [ph.enter_context(nc.psum_tensor(_u("at_ps_s"), [128, TG], F32)) for i in range(4)]
            ps_c = [ph.enter_context(nc.psum_tensor(_u("at_ps_c"), [65, TG], F32)) for i in range(2)]

            r0 = hc * 128
            k.ld(q2[:], qT[r0:r0 + 128, :], queue="s")
            k.ld(k2[:], kT[r0:r0 + 128, :], queue="s")
            k.ld(v2[:], vT[r0:r0 + 128, :], queue="s")
            fl0 = k.flush("s")

            mevs = [k.V(lambda v, h=h: v.memset(vtr[h][:], 1.0), inc=True) for h in range(2)]
            prev_evac = None
            for kt in range(NT):
                for h in range(2):
                    tre = k.T(lambda t, kt=kt, h=h: t.transpose(
                        ps_tr[:, 0:64], v2[h * 64:(h + 1) * 64, kt * 128:(kt + 1) * 128],
                        ident[h * 64:(h + 1) * 64, h * 64:(h + 1) * 64]),
                        waits=[fl0, prev_evac] + mevs, inc=True)
                    prev_evac = k.S(lambda s, kt=kt, h=h: s.copy(
                        vtr[h][:, kt * 65:kt * 65 + 64], ps_tr[:, 0:64]),
                        waits=[tre] + mevs, inc=True, drain=True)
            vtr_ev = prev_evac

            def norms(srcbuf, col, extra_wait):
                sqe = k.S(lambda s: s.activation(out=sq[:], in_=srcbuf[:], func=AF.Square),
                          waits=[fl0, extra_wait], inc=True, drain=True)
                rd = None
                last_mm = None
                for g in range(NG):
                    me = k.T(lambda t, g=g: t.matmul(
                        ps_n[:, :], ind2[:], sq[:, g * TG:(g + 1) * TG],
                        start=True, stop=True), waits=[sqe, rd], inc=True)
                    last_mm = me
                    rd = k.V(lambda v, g=g: v.reduce_max(nrm[:, g:g + 1], ps_n[:], axis=AX),
                             waits=[me], inc=True, drain=True)
                e = k.V(lambda v, col=col: v.reduce_max(brow[:, col:col + 1], nrm[:], axis=AX),
                        waits=[rd], inc=True, drain=True)
                return e, last_mm
            eq_n, qmm = norms(q2, 0, None)
            ek_n, _ = norms(k2, 1, qmm)
            e_sq = k.S(lambda s: s.activation(out=brow[:, 0:2], in_=brow[:, 0:2], func=AF.Sqrt),
                       waits=[eq_n, ek_n], inc=True, drain=True)
            k.V(lambda v: v.tensor_mul(brow[:, 2:3], brow[:, 0:1], brow[:, 1:2]),
                waits=[e_sq], drain=True)
            e_b = k.V(lambda v: v.tensor_scalar(brow[:, 3:4], brow[:, 2:3], -1.0, None,
                                                op0=ALU.mult), inc=True, drain=True)
            k.ld(drow[2:3, 0:2], brow[:, 3:4], waits=[e_b], queue="s")
            fb1 = k.flush("s")
            for h in range(2):
                k.ld(negb[h][:], drow[2:3, h:h + 1].partition_broadcast(128),
                     waits=[fb1], queue="s")
            if heads_z2d is not None:
                for h in range(2):
                    k.ld(hzb[h][:], heads_z2d[0:1, hc * 2 + h:hc * 2 + h + 1]
                         .partition_broadcast(64), queue="s")
            fb2 = k.flush("s")

            ssr = Ring(4)
            pvr = [None, None]
            octx_ok = [None, None]
            for qc in range(NG):
                qs_ = slice(qc * TG, (qc + 1) * TG)
                ktmax = NT if not causal else (qc + 1) * (TG // 128)
                exp_done = [None, None]
                for h in range(2):
                    for kt in range(ktmax):
                        pi, pev = ssr.nxt()
                        ps = ps_s[pi]
                        me = k.T(lambda t, ps=ps, h=h, kt=kt, qs_=qs_: t.matmul(
                            ps[:], k2[h * 64:(h + 1) * 64, kt * 128:(kt + 1) * 128],
                            q2[h * 64:(h + 1) * 64, qs_], start=True, stop=True),
                            waits=[fl0, pev, pvr[h]] if kt == 0 else [pev],
                            inc=True)
                        ee = k.S(lambda s, ps=ps, h=h, kt=kt: s.activation(
                            out=pT[h][:, kt * TG:(kt + 1) * TG], in_=ps[:], func=AF.Exp,
                            bias=negb[h][:]), waits=[me, fb2], inc=True, drain=True)
                        if causal and kt >= qc * (TG // 128):
                            ee = k.G(lambda g, h=h, kt=kt, qc=qc: g.affine_select(
                                out=pT[h][:, kt * TG:(kt + 1) * TG],
                                in_=pT[h][:, kt * TG:(kt + 1) * TG],
                                pattern=[[1, TG]], compare_op=ALU.is_ge, fill=0.0,
                                base=qc * TG - kt * 128, channel_multiplier=-1),
                                waits=[ee], inc=True, drain=True)
                        ssr.done(ee)
                        exp_done[h] = ee
                for h in range(2):
                    last_pv = None
                    for kt in range(ktmax):
                        last_pv = k.T(lambda t, h=h, kt=kt, ktmax=ktmax: t.matmul(
                            ps_c[h][:], vtr[h][:, kt * 65:(kt + 1) * 65],
                            pT[h][:, kt * TG:(kt + 1) * TG],
                            start=(kt == 0), stop=(kt == ktmax - 1)),
                            waits=[exp_done[h], octx_ok[h], vtr_ev] if kt == 0 else (),
                            inc=(kt == ktmax - 1))
                    pvr[h] = last_pv
                    re = k.V(lambda v, h=h: v.reciprocal(recr[64:65, :], ps_c[h][64:65, :]),
                             waits=[last_pv], inc=True, drain=True)
                    k.ld(drow[3 + h:4 + h, 0:TG], recr[64:65, :], waits=[re], queue="s")
                    fr1 = k.flush("s")
                    k.ld(recb[h][:], drow[3 + h:4 + h, 0:TG].partition_broadcast(64),
                         waits=[fr1], queue="s")
                    fr2 = k.flush("s")
                    ce = k.S(lambda s, h=h: s.copy(sctx[h][:], ps_c[h][0:64, :]),
                             waits=[last_pv], inc=True, drain=True)
                    me1 = k.V(lambda v, h=h: v.tensor_mul(octx[h][:], sctx[h][:], recb[h][:]),
                              waits=[fr2, ce, octx_ok[h]], inc=True, drain=True)
                    if heads_z2d is not None:
                        me1 = k.V(lambda v, h=h: v.tensor_scalar(
                            octx[h][:], octx[h][:], hzb[h][:], None, op0=ALU.mult),
                            inc=True, drain=True)
                    k.ld(ctxT[r0 + h * 64:r0 + (h + 1) * 64, qs_], octx[h][:],
                         waits=[me1], queue="s")
                    octx_ok[h] = k.flush("s")

    for hc in range(nheads // 2):
        with contextlib.ExitStack() as ph:
            _hc_body(hc, ph)
        k.barrier()


def emit_rope_il(k, bufT):
    nc = k.nc
    rot = k.consts_sb["rot_il_bf"]
    with contextlib.ExitStack() as ph:
        S = lambda n, sh, dt: ph.enter_context(nc.sbuf_tensor(_u(n), sh, dt))
        cosT = S("rp_cos", [128, T], F32)
        sinT = S("rp_sin", [128, T], F32)
        tq = [S(f"rp_q{i}", [128, T], BF16) for i in range(2)]
        to = [S(f"rp_o{i}", [128, T], BF16) for i in range(2)]
        t1 = S("rp_t1", [128, TG], F32)
        t2 = S("rp_t2", [128, TG], F32)
        pss = [ph.enter_context(nc.psum_tensor(_u("rp_ps"), [128, TG], F32)) for i in range(2)]
        k.ld(cosT[:], k.dconst("ropeA_cos"), queue="s")
        k.ld(sinT[:], k.dconst("ropeA_sin"), queue="s")
        psr = Ring(2)
        chunk_flush = [None, None]
        for c in range(NCH):
            bi = c % 2
            k.ld(tq[bi][:], bufT[c * 128:(c + 1) * 128, :], waits=[chunk_flush[bi]], queue="s")
            fl = k.flush("s")
            last = None
            for g in range(NG):
                gs = slice(g * TG, (g + 1) * TG)
                pi, pev = psr.nxt()
                ps = pss[pi]
                me = k.T(lambda t, ps=ps, bi=bi, gs=gs: t.matmul(
                    ps[:], rot[:], tq[bi][:, gs], start=True, stop=True),
                    waits=[fl, pev], inc=True)
                k.V(lambda v, bi=bi, gs=gs: v.tensor_mul(t1[:], tq[bi][:, gs], cosT[:, gs]),
                    waits=[fl, ("vD", k.cnt["vD"])], drain=True)
                k.V(lambda v, ps=ps, gs=gs: v.tensor_mul(t2[:], ps[:], sinT[:, gs]),
                    waits=[me], drain=True)
                last = k.V(lambda v, bi=bi, gs=gs: v.tensor_add(to[bi][:, gs], t1[:], t2[:]),
                           waits=[chunk_flush[bi]], inc=True, drain=True)
                psr.done(last)
            k.ld(bufT[c * 128:(c + 1) * 128, :], to[bi][:], waits=[last], queue="s")
            chunk_flush[bi] = k.flush("s")
    k.barrier()


MOE_DBG = [None]

def emit_moe(k, xln_bf, rwT, EW, xin, xout):
    nc = k.nc
    ident = k.consts_sb["ident_f32"]
    NT = T // 128
    with contextlib.ExitStack() as ph:
        S = lambda n, sh, dt: ph.enter_context(nc.sbuf_tensor(_u(n), sh, dt))
        Wsb = S("mo_w", [128, 8 * 2 * 8 * 512], BF16)
        rw = S("mo_rw", [128, 8 * 8], BF16)
        xlt = [S(f"mo_x{i}", [128, 8 * 128], BF16) for i in range(2)]
        lgtb = S("mo_lg", [128, 48 * NT], F32)
        srowb = S("mo_sr", [128, 8 * NT], F32)
        acc = [S(f"mo_acc{i}", [128, 1024], F32) for i in range(2)]
        xr = [S(f"mo_xr{i}", [128, 128], F32) for i in range(8)]
        rt = [S(f"mo_rt{i}", [128, 128], F32) for i in range(8)]
        ps_e = [ph.enter_context(nc.psum_tensor(_u("mo_ps"), [128, TG], F32)) for i in range(8)]

        for e in range(8):
            for fg in range(2):
                base = (e * 2 + fg) * 8 * 512
                k.ld(Wsb[:, base:base + 8 * 512].rearrange("p (c j) -> p c j", j=512),
                     EW[e].rearrange("(c p) f -> p c f", p=128)[:, :, fg * 512:(fg + 1) * 512],
                     queue="g")
        k.ld(rw[:].rearrange("p (c e) -> p c e", e=8),
             rwT.rearrange("(c p) e -> p c e", p=128), queue="g")
        wfl = k.flush("g")

        xlr = Ring(2)
        tile_flush = None
        for tt in range(NT):
            ts_ = slice(tt * 128, (tt + 1) * 128)
            xi, xev_old = xlr.nxt()
            xl = xlt[xi]
            for c in range(NCH):
                k.ld(xl[:, c * 128:(c + 1) * 128], xln_bf[c * 128:(c + 1) * 128, ts_],
                     waits=[xev_old], queue="s")
            for j in range(8):
                k.ld(xr[j][:], xin[j * 128:(j + 1) * 128, ts_],
                     waits=[tile_flush], queue="s")
            lfl = k.flush("s")
            ps_r = ps_e[0][:, 0:8]
            lgt = lgtb[:, tt * 48:(tt + 1) * 48]
            srow = srowb[:, tt * 8:(tt + 1) * 8]
            rme = None
            vs = ("vD", k.cnt["vD"])
            for c in range(NCH):
                rme = k.T(lambda t, c=c, xl=xl: t.matmul(
                    ps_r, xl[:, c * 128:(c + 1) * 128], rw[:, c * 8:(c + 1) * 8],
                    start=(c == 0), stop=(c == NCH - 1)),
                    waits=[lfl, wfl, vs] if c == 0 else (), inc=True)
            l8 = lgt[:, 0:8]; p8 = lgt[:, 8:16]; eq8 = lgt[:, 16:24]
            q8 = lgt[:, 24:32]; ge8 = lgt[:, 32:40]; cf8 = lgt[:, 40:48]
            negm = srow[:, 0:1]; esum = srow[:, 1:2]; rec = srow[:, 2:3]
            M1 = srow[:, 3:4]; M2 = srow[:, 4:5]; s12 = srow[:, 5:6]; beta = srow[:, 6:7]
            k.V(lambda v, l8=l8, ps_r=ps_r: v.tensor_copy(l8, ps_r), waits=[rme], drain=True)
            nme = k.V(lambda v, negm=negm, l8=l8: v.reduce_max(negm, l8, axis=AX, negate=True),
                      inc=True, drain=True)
            se = k.S(lambda s, p8=p8, l8=l8, negm=negm, esum=esum: s.activation(
                     out=p8, in_=l8, func=AF.Exp, bias=negm, accum_out=esum),
                     waits=[nme], inc=True, drain=True)
            k.V(lambda v, rec=rec, esum=esum: v.reciprocal(rec, esum), waits=[se], drain=True)
            k.V(lambda v, p8=p8, rec=rec: v.tensor_scalar(p8, p8, rec, None, op0=ALU.mult), drain=True)
            k.V(lambda v, M1=M1, p8=p8: v.reduce_max(M1, p8, axis=AX), drain=True)
            k.V(lambda v, eq8=eq8, p8=p8, M1=M1: v.tensor_scalar(eq8, p8, M1, None, op0=ALU.is_equal), drain=True)
            k.V(lambda v, q8=q8, eq8=eq8, p8=p8: v.scalar_tensor_tensor(q8, eq8, -1e30, p8, op0=ALU.mult, op1=ALU.add), drain=True)
            k.V(lambda v, M2=M2, q8=q8: v.reduce_max(M2, q8, axis=AX), drain=True)
            k.V(lambda v, ge8=ge8, p8=p8, M2=M2: v.tensor_scalar(ge8, p8, M2, None, op0=ALU.is_ge), drain=True)
            k.V(lambda v, s12=s12, M1=M1, M2=M2: v.tensor_add(s12, M1, M2), drain=True)
            k.V(lambda v, s12=s12: v.tensor_scalar(s12, s12, 1e-6, None, op0=ALU.add), drain=True)
            k.V(lambda v, rec=rec, s12=s12: v.reciprocal(rec, s12), drain=True)
            k.V(lambda v, cf8=cf8, p8=p8, ge8=ge8: v.tensor_mul(cf8, p8, ge8), drain=True)
            k.V(lambda v, cf8=cf8, rec=rec: v.tensor_scalar(cf8, cf8, rec, None, op0=ALU.mult), drain=True)
            k.V(lambda v, q8=q8, cf8=cf8, p8=p8: v.tensor_mul(q8, cf8, p8), drain=True)
            ce = k.V(lambda v, beta=beta, q8=q8: v.reduce_sum(beta, q8, axis=AX), inc=True, drain=True)
            accev = [None, None]
            emms = []
            for fg in range(2):
                ac = acc[fg]
                emms = []
                vs2 = ("vD", k.cnt["vD"])
                for c in range(NCH):
                    for e in range(8):
                        base = (e * 2 + fg) * 8 * 512
                        me = k.T(lambda t, c=c, e=e, base=base, xl=xl: t.matmul(
                            ps_e[e][:], xl[:, c * 128:(c + 1) * 128],
                            Wsb[:, base + c * 512:base + (c + 1) * 512],
                            start=(c == 0), stop=(c == NCH - 1)),
                            waits=[vs2, ce] if c == 0 else (), inc=(c == NCH - 1))
                        if c == NCH - 1:
                            emms.append(me)
                k.V(lambda v, ac=ac, cf8=cf8: v.tensor_scalar(
                    ac[:, 0:512], ps_e[0][:], cf8[:, 0:1], None, op0=ALU.mult),
                    waits=[emms[0], ce, tile_flush], drain=True)
                for e in range(1, 8):
                    k.V(lambda v, e=e, ac=ac, cf8=cf8: v.scalar_tensor_tensor(
                        ac[:, 0:512], ps_e[e][:], cf8[:, e:e + 1], ac[:, 0:512],
                        op0=ALU.mult, op1=ALU.add),
                        waits=[emms[e]], drain=True)
                accev[fg] = k.V(lambda v, ac=ac, beta=beta: v.tensor_scalar(
                    ac[:, 0:512], ac[:, 0:512], beta, None, op0=ALU.add),
                    inc=True, drain=True)
            if MOE_DBG[0] is not None:
                k.ld(MOE_DBG[0][:, tt * 56:tt * 56 + 48], lgt[:], waits=[ce], queue="s")
                k.ld(MOE_DBG[0][:, tt * 56 + 48:tt * 56 + 56], srow[:], waits=[ce], queue="s")
            prev_ae = [None, None]
            for j in range(8):
                fg, jj = divmod(j, 4)
                ac = acc[fg]
                pst = ps_e[1 + (j % 2)]
                te = k.T(lambda t, ac=ac, jj=jj, pst=pst: t.transpose(
                    pst[0:128, 0:128], ac[:, jj * 128:(jj + 1) * 128], ident[:]),
                    waits=[accev[1], prev_ae[j % 2]], inc=True)
                ae = k.V(lambda v, j=j, pst=pst: v.tensor_add(
                    rt[j][:], pst[0:128, 0:128], xr[j][:]),
                    waits=[te, lfl, tile_flush], inc=True, drain=True)
                prev_ae[j % 2] = ae
                k.ld(xout[j * 128:(j + 1) * 128, ts_], rt[j][:], waits=[ae], queue="s")
            tile_flush = k.flush("s")
            xlr.done(emms[-1])
    k.barrier()


def emit_attn_b(k, qkvT, xin, xout):
    nc = k.nc
    ind8 = k.consts_sb["indrep8_bf"]
    with contextlib.ExitStack() as ph:
        S = lambda n, sh, dt: ph.enter_context(nc.sbuf_tensor(_u(n), sh, dt))
        qc_ = S("ab_q", [128, T], BF16)
        kc_ = S("ab_k", [128, T], BF16)
        vc_ = S("ab_v", [128, T], BF16)
        xr = S("ab_xr", [128, T], F32)
        Cs = S("ab_C", [128, T], F32)
        zb = S("ab_z", [128, T], F32)
        iot = S("ab_iot", [128, T], F32)
        qkb = [S(f"ab_qk{i}", [128, TG], BF16) for i in range(2)]
        ab = S("ab_a", [128, TG], F32)
        bb = S("ab_b", [128, TG], F32)
        db = S("ab_d", [128, TG], F32)
        rb = S("ab_r", [128, TG], F32)
        nb = S("ab_n", [128, TG], F32)
        t2 = S("ab_t2", [128, TG], F32)
        ob = [S(f"ab_o{i}", [128, TG], F32) for i in range(4)]
        pss = [ph.enter_context(nc.psum_tensor(_u("ab_ps"), [128, TG], F32)) for i in range(2)]

        ze = k.V(lambda v: v.memset(zb[:], 0.0), inc=True, drain=True)
        ie = k.G(lambda g: g.iota(iot[:], pattern=[[1, T]], base=0, channel_multiplier=0,
                                  allow_small_or_imprecise_dtypes=True),
                 inc=True, drain=True)
        chunk_flush = None
        for c in range(NCH):
            k.ld(qc_[:], qkvT[c * 128:(c + 1) * 128, :], waits=[chunk_flush], queue="s")
            k.ld(kc_[:], qkvT[1024 + c * 128:1024 + (c + 1) * 128, :], queue="s")
            k.ld(vc_[:], qkvT[2048 + c * 128:2048 + (c + 1) * 128, :], queue="s")
            k.ld(xr[:], xin[c * 128:(c + 1) * 128, :], queue="s")
            fl = k.flush("s")
            sce = k.V(lambda v: v.tensor_tensor_scan(Cs[:], vc_[:], zb[:], 0.0,
                                                     op0=ALU.add, op1=ALU.bypass),
                      waits=[fl, ze, chunk_flush], inc=True, drain=True)
            qkr = Ring(2)
            psr = Ring(2)
            str_ = Ring(4)
            for g in range(NG):
                gs = slice(g * TG, (g + 1) * TG)
                qi, qev = qkr.nxt()
                qke = k.V(lambda v, qi=qi, gs=gs: v.tensor_mul(qkb[qi][:], qc_[:, gs], kc_[:, gs]),
                          waits=[fl, qev], inc=True, drain=True)
                pi, pev = psr.nxt()
                ps = pss[pi]
                me = k.T(lambda t, ps=ps, qi=qi: t.matmul(
                    ps[:], ind8[:], qkb[qi][:], start=True, stop=True),
                    waits=[qke, pev], inc=True)
                qkr.done(me)
                k.V(lambda v, ps=ps: v.tensor_scalar(ab[:], ps[:], 0.0, None, op0=ALU.min),
                    waits=[me, ("sD", k.cnt["sD"])], drain=True)
                ea = k.V(lambda v, ps=ps: v.tensor_scalar(bb[:], ps[:], -1.0, 0.0,
                                                          op0=ALU.mult, op1=ALU.min),
                         inc=True, drain=True)
                eb = k.S(lambda s: s.activation(out=ab[:], in_=ab[:], func=AF.Exp),
                         waits=[ea], drain=True)
                eb = k.S(lambda s: s.activation(out=bb[:], in_=bb[:], func=AF.Exp),
                         inc=True, drain=True)
                k.V(lambda v, gs=gs: v.tensor_mul(db[:], iot[:, gs], bb[:]),
                    waits=[eb, ie], drain=True)
                k.V(lambda v: v.tensor_add(db[:], db[:], ab[:]), drain=True)
                k.V(lambda v: v.reciprocal(rb[:], db[:]), drain=True)
                k.V(lambda v: v.tensor_sub(ab[:], ab[:], bb[:]), drain=True)
                k.V(lambda v, gs=gs: v.tensor_mul(nb[:], ab[:], vc_[:, gs]), drain=True)
                k.V(lambda v, gs=gs: v.tensor_mul(t2[:], bb[:], Cs[:, gs]),
                    waits=[sce], drain=True)
                k.V(lambda v: v.tensor_add(nb[:], nb[:], t2[:]), drain=True)
                si, sev = str_.nxt()
                o = ob[si]
                k.V(lambda v, o=o: v.tensor_mul(o[:], nb[:], rb[:]),
                    waits=[sev], drain=True)
                oe = k.V(lambda v, o=o, gs=gs: v.tensor_add(o[:], o[:], xr[:, gs]),
                         inc=True, drain=True)
                k.ld(xout[c * 128:(c + 1) * 128, gs], o[:], waits=[oe], queue="s")
                str_.done(None)
            chunk_flush = k.flush("s")
    k.barrier()


def emit_l2rope(k, bufT, row0, scale2d, cos_key, sin_key):
    nc = k.nc
    indrep = k.consts_sb["indrep_bf"]
    rot = k.consts_sb["rot_hs_bf"]
    with contextlib.ExitStack() as ph:
        S = lambda n, sh, dt: ph.enter_context(nc.sbuf_tensor(_u(n), sh, dt))
        cosT = S("lr_cos", [128, T], F32)
        sinT = S("lr_sin", [128, T], F32)
        tq = [S(f"lr_q{i}", [128, T], BF16) for i in range(2)]
        to = [S(f"lr_o{i}", [128, T], BF16) for i in range(2)]
        sqb = S("lr_sq", [128, TG], BF16)
        nbuf = S("lr_n", [128, TG], F32)
        qhb = S("lr_qh", [128, TG], BF16)
        t1 = S("lr_t1", [128, TG], F32)
        t2 = S("lr_t2", [128, TG], F32)
        qs1 = S("lr_qs", [128, 1], F32)
        ps_n = ph.enter_context(nc.psum_tensor(_u("lr_psn"), [128, TG], F32))
        ps_r = ph.enter_context(nc.psum_tensor(_u("lr_psr"), [128, TG], F32))
        k.ld(cosT[:], k.dconst(cos_key), queue="s")
        k.ld(sinT[:], k.dconst(sin_key), queue="s")
        k.ld(qs1[:], scale2d, queue="s")
        chunk_flush = [None, None]
        for c in range(4):
            bi = c % 2
            k.ld(tq[bi][:], bufT[row0 + c * 128:row0 + (c + 1) * 128, :],
                 waits=[chunk_flush[bi]], queue="s")
            fl = k.flush("s")
            last = None
            nme_prev = None
            rme_prev = None
            for g in range(NG):
                gs = slice(g * TG, (g + 1) * TG)
                sqe = k.S(lambda s, bi=bi, gs=gs: s.activation(out=sqb[:], in_=tq[bi][:, gs],
                                                               func=AF.Square),
                          waits=[fl, nme_prev], inc=True, drain=True)
                nme = k.T(lambda t: t.matmul(ps_n[:], indrep[:], sqb[:], start=True, stop=True),
                          waits=[sqe, ("vD", k.cnt["vD"])], inc=True)
                nme_prev = nme
                k.V(lambda v: v.tensor_scalar(nbuf[:], ps_n[:], 1e-24, None, op0=ALU.max),
                    waits=[nme, ("sD", k.cnt["sD"])], drain=True)
                e1 = k.V(lambda v: v.engine_nop(), inc=True)
                k.S(lambda s: s.activation(out=nbuf[:], in_=nbuf[:], func=AF.Ln),
                    waits=[e1], drain=True)
                e2 = k.S(lambda s: s.activation(out=nbuf[:], in_=nbuf[:], func=AF.Exp,
                                                scale=-0.5), inc=True, drain=True)
                k.V(lambda v, bi=bi, gs=gs: v.tensor_mul(t1[:], tq[bi][:, gs], nbuf[:]),
                    waits=[e2], drain=True)
                e3 = k.V(lambda v: v.tensor_scalar(qhb[:], t1[:], qs1[:], None, op0=ALU.mult),
                         waits=[rme_prev], inc=True, drain=True)
                rme = k.T(lambda t: t.matmul(ps_r[:], rot[:], qhb[:], start=True, stop=True),
                          waits=[e3], inc=True)
                rme_prev = rme
                k.V(lambda v, gs=gs: v.tensor_mul(t1[:], qhb[:], cosT[:, gs]),
                    waits=[e3], drain=True)
                k.V(lambda v, gs=gs: v.tensor_mul(t2[:], ps_r[:], sinT[:, gs]),
                    waits=[rme], drain=True)
                last = k.V(lambda v, bi=bi, gs=gs: v.tensor_add(to[bi][:, gs], t1[:], t2[:]),
                           waits=[chunk_flush[bi]], inc=True, drain=True)
            k.ld(bufT[row0 + c * 128:row0 + (c + 1) * 128, :], to[bi][:],
                 waits=[last], queue="s")
            chunk_flush[bi] = k.flush("s")
    k.barrier()


def emit_lattn(k, qkvT, outT):
    nc = k.nc
    ident = k.consts_sb["ident_bf"]
    NT = T // 128
    NW = T // 512
    drow = k.drows
    with contextlib.ExitStack() as ph:
        S = lambda n, sh, dt: ph.enter_context(nc.sbuf_tensor(_u(n), sh, dt))
        mskT = S("la_mskT", [128, 8 * 512], F32)
        k.ld(mskT[:], k.dconst("lmaskT"), queue="s")
        mfl = k.flush("s")

        def _hc_body(hc, ph2):
            S2 = lambda n, sh, dt: ph2.enter_context(nc.sbuf_tensor(_u(n), sh, dt))
            q2 = S2("la_q2", [128, T], BF16)
            k2 = S2("la_k2", [128, T], BF16)
            v2 = S2("la_v2", [128, T], BF16)
            vtr = [S2(f"la_vtr{h}", [128, NT * 65], BF16) for h in range(2)]
            Ssb = [S2(f"la_S{i}", [128, 8 * 512], F32) for i in range(2)]
            pTb = [S2(f"la_pT{i}", [128, 8 * 512], BF16) for i in range(2)]
            cmax8 = [S2(f"la_cm8{i}", [128, 8], F32) for i in range(2)]
            cm2 = S2("la_cm2", [128, 4], F32)
            cmb = [S2(f"la_cmb{i}", [128, 256], F32) for i in range(2)]
            negB = S2("la_negB", [128, 4], F32)
            octf = S2("la_octf", [128, T], F32)
            octb = S2("la_octb", [128, T], BF16)
            den128 = S2("la_den", [128, T], F32)
            denst = S2("la_denst", [65, 4 * 512], F32)
            ps_tr = [ph2.enter_context(nc.psum_tensor(_u("la_ptr"), [128, 64], BF16)) for i in range(2)]
            ps_s = [ph2.enter_context(nc.psum_tensor(_u("la_ps"), [128, TG], F32)) for i in range(4)]
            ps_c = [ph2.enter_context(nc.psum_tensor(_u("la_pc"), [65, TG], F32)) for i in range(2)]

            k.ld(q2[:], qkvT[hc * 128:(hc + 1) * 128, :], queue="s")
            k.ld(k2[:], qkvT[512 + hc * 128:512 + (hc + 1) * 128, :], queue="s")
            k.ld(v2[:], qkvT[1024 + hc * 128:1024 + (hc + 1) * 128, :], queue="s")
            fl0 = k.flush("s")
            mev = [k.V(lambda v, h=h: v.memset(vtr[h][:], 1.0), inc=True) for h in range(2)]
            trr = [None, None]
            for kt in range(NT):
                for h in range(2):
                    tre = k.T(lambda t, kt=kt, h=h: t.transpose(
                        ps_tr[h][:, 0:64], v2[h * 64:(h + 1) * 64, kt * 128:(kt + 1) * 128],
                        ident[h * 64:(h + 1) * 64, h * 64:(h + 1) * 64]),
                        waits=[fl0, trr[h]] + mev, inc=True)
                    trr[h] = k.S(lambda s, kt=kt, h=h: s.copy(
                        vtr[h][:, kt * 65:kt * 65 + 64], ps_tr[h][:, 0:64]),
                        waits=[tre], inc=True, drain=True)

            ssb_free = [None, None]
            ptb_free = [None, None]
            psf = [None] * 4
            evac_ev = [None, None]
            dce_ev = [None, None]
            cm_free = [None, None]
            negb_free = [None] * 4
            dfl_ev = None
            last_den = None
            for w in range(NW):
                pw = w % 2
                kt0 = 4 if w == 0 else 0
                qs_ = slice(w * 512, (w + 1) * 512)
                if w > 0:
                    dfl_ev = k.flush("g")
                cm_evs = []
                stg_last = [None, None]
                for h in range(2):
                    g = w * 2 + h
                    par = g % 2
                    for kt in range(kt0, 8):
                        b = (kt + 4 * h) % 4
                        klo = ((w - 1) * 4 + kt) * 128
                        mm = k.T(lambda t, b=b, h=h, klo=klo, qs_=qs_: t.matmul(
                            ps_s[b][:], k2[h * 64:(h + 1) * 64, klo:klo + 128],
                            q2[h * 64:(h + 1) * 64, qs_], start=True, stop=True),
                            waits=[fl0, psf[b]], inc=True)
                        st = k.V(lambda v, b=b, kt=kt, par=par: v.tensor_add(
                            Ssb[par][:, kt * 512:(kt + 1) * 512], ps_s[b][:],
                            mskT[:, kt * 512:(kt + 1) * 512]),
                            waits=[mm, mfl, ssb_free[par] if kt < kt0 + 2 else None],
                            inc=True, drain=True)
                        k.V(lambda v, kt=kt, par=par: v.reduce_max(
                            cmax8[par][:, kt:kt + 1],
                            Ssb[par][:, kt * 512:(kt + 1) * 512], axis=AX), drain=True)
                        psf[b] = st
                        stg_last[h] = st
                    rde = k.V(lambda v, pw=pw, h=h, par=par, kt0=kt0: v.reduce_max(
                        cm2[:, pw * 2 + h:pw * 2 + h + 1], cmax8[par][:, kt0:8], axis=AX),
                        waits=[cm_free[pw]], inc=True, drain=True)
                    cm_evs.append(rde)
                k.ld(drow[4 + pw:5 + pw, 0:128], cm2[:, pw * 2:pw * 2 + 1],
                     waits=[cm_evs[0]], queue="s")
                k.ld(drow[4 + pw:5 + pw, 128:256], cm2[:, pw * 2 + 1:pw * 2 + 2],
                     waits=[cm_evs[1]], queue="s")
                f1 = k.flush("s")
                k.ld(cmb[pw][:], drow[4 + pw:5 + pw, 0:256].partition_broadcast(128),
                     waits=[f1], queue="s")
                f2 = k.flush("s")
                cm_free[pw] = f2
                nbe = {}
                for h in range(2):
                    nbe[h] = k.V(lambda v, pw=pw, h=h: v.reduce_max(
                        negB[:, pw * 2 + h:pw * 2 + h + 1], cmb[pw][:, h * 128:(h + 1) * 128],
                        axis=AX, negate=True),
                        waits=[f2, negb_free[pw * 2 + h]], inc=True, drain=True)
                for h in range(2):
                    g = w * 2 + h
                    par = g % 2
                    ex = k.S(lambda s, par=par, kt0=kt0, pw=pw, h=h: s.activation(
                        out=pTb[par][:, kt0 * 512:], in_=Ssb[par][:, kt0 * 512:],
                        func=AF.Exp, bias=negB[:, pw * 2 + h:pw * 2 + h + 1]),
                        waits=[nbe[h], stg_last[h], ptb_free[par]], inc=True, drain=True)
                    ssb_free[par] = ex
                    negb_free[pw * 2 + h] = ex
                    last_pv = None
                    for kt in range(kt0, 8):
                        ktg = (w - 1) * 4 + kt
                        last_pv = k.T(lambda t, h=h, kt=kt, ktg=ktg, kt0=kt0, par=par:
                            t.matmul(ps_c[h][:], vtr[h][:, ktg * 65:(ktg + 1) * 65],
                                     pTb[par][:, kt * 512:(kt + 1) * 512],
                                     start=(kt == kt0), stop=(kt == 7)),
                            waits=[ex, evac_ev[h], dce_ev[h], dfl_ev] if kt == kt0 else (),
                            inc=(kt == 7))
                    ptb_free[par] = last_pv
                    dslot = slice((pw * 2 + h) * 512, (pw * 2 + h + 1) * 512)
                    dce = k.S(lambda s, h=h, dslot=dslot: s.copy(
                        denst[64:65, dslot], ps_c[h][64:65, :]),
                        waits=[last_pv, dfl_ev], inc=True, drain=True)
                    k.ld(drow[6 + h:7 + h, qs_], denst[64:65, dslot],
                         waits=[dce], queue="g")
                    dce_ev[h] = dce
                    evac_ev[h] = k.V(lambda v, h=h, qs_=qs_: v.tensor_copy(
                        octf[h * 64:(h + 1) * 64, qs_], ps_c[h][0:64, :]),
                        waits=[last_pv], inc=True, drain=True)
            fgl = k.flush("g")
            for h in range(2):
                k.ld(den128[h * 64:(h + 1) * 64, :],
                     drow[6 + h:7 + h, 0:T].partition_broadcast(64),
                     waits=[fgl], queue="s")
            fl2 = k.flush("s")
            k.V(lambda v: v.reciprocal(den128[:], den128[:]), waits=[fl2], drain=True)
            dv = [k.V(lambda v: v.tensor_mul(octb[:], octf[:], den128[:]),
                      waits=[evac_ev[0], evac_ev[1]], inc=True, drain=True)]
            k.ld(outT[hc * 128:(hc + 1) * 128, :], octb[:], waits=dv, queue="s")
            k.flush("s")

        for hc in range(4):
            with contextlib.ExitStack() as ph2:
                _hc_body(hc, ph2)
            k.barrier()
    k.barrier()



def emit_glu(k, src_bf, W, b1a2d, b1g2d, dst):
    nc = k.nc
    HID = 2730
    npair = (HID + 127) // 128
    with contextlib.ExitStack() as ph:
        S = lambda n, sh, dt: ph.enter_context(nc.sbuf_tensor(_u(n), sh, dt))
        XL = [S(f"gl_x{c}", [128, T], BF16) for c in range(NCH)]
        WA = [S(f"gl_wa{i}", [128, 128 * NCH], BF16) for i in range(2)]
        WG = [S(f"gl_wg{i}", [128, 128 * NCH], BF16) for i in range(2)]
        ba = S("gl_ba", [128, npair], F32)
        bg = S("gl_bg", [128, npair], F32)
        gg = [S(f"gl_gg{i}", [128, TG], BF16) for i in range(2)]
        aa = [S(f"gl_aa{i}", [128, TG], BF16) for i in range(2)]
        stage = [S(f"gl_s{i}", [128, TG], BF16) for i in range(4)]
        pss = [ph.enter_context(nc.psum_tensor(_u("gl_ps"), [128, TG], F32)) for i in range(4)]

        for c in range(NCH):
            k.ld(XL[c][:], src_bf[c * 128:(c + 1) * 128, :], queue="s")
        k.ld(ba[:], b1a2d, queue="s")
        k.ld(bg[:], b1g2d, queue="s")
        flx = k.flush("s")
        Wr = W.rearrange("(c p) f -> p c f", p=128)
        wring = Ring(2)
        psr = Ring(2)
        psgr = Ring(2)
        ggr = Ring(2)
        str_ = Ring(4)
        ftile_flush = None
        for fp in range(npair):
            wb, wev_old = wring.nxt()
            flo = fp * 128
            fw = min(HID - flo, 128)
            k.ld(WA[wb][:, 0:NCH * fw].rearrange("p (c j) -> p c j", j=fw),
                 Wr[:, 0:NCH, flo:flo + fw], waits=[wev_old], queue="g")
            k.ld(WG[wb][:, 0:NCH * fw].rearrange("p (c j) -> p c j", j=fw),
                 Wr[:, 0:NCH, HID + flo:HID + flo + fw], waits=[wev_old], queue="g")
            wev = k.flush("g")
            f_last = None
            for g in range(NG):
                gs = slice(g * TG, (g + 1) * TG)
                pi, pev = psr.nxt()
                pgi, pgev = psgr.nxt()
                psA = pss[pi]
                psG = pss[2 + pgi]
                for c in range(NCH):
                    k.T(lambda t, psA=psA, wb=wb, c=c, fw=fw, gs=gs:
                        t.matmul(psA[0:fw, :], WA[wb][:, c * fw:(c + 1) * fw],
                                 XL[c][:, gs], start=(c == 0), stop=(c == NCH - 1)),
                        waits=[wev, pev, flx] if c == 0 else ())
                    mm = k.T(lambda t, psG=psG, wb=wb, c=c, fw=fw, gs=gs:
                             t.matmul(psG[0:fw, :], WG[wb][:, c * fw:(c + 1) * fw],
                                      XL[c][:, gs], start=(c == 0), stop=(c == NCH - 1)),
                             waits=[wev, pgev, flx] if c == 0 else (),
                             inc=(c == NCH - 1))
                f_last = mm
                gi, gev = ggr.nxt()
                ge_ = k.S(lambda s, psG=psG, gi=gi, fw=fw, fp=fp: s.activation(
                    out=gg[gi][0:fw, :], in_=psG[0:fw, :], func=AF.Gelu,
                    bias=bg[0:fw, fp:fp + 1]), waits=[mm, gev], inc=True, drain=True)
                ae_ = k.V(lambda v, psA=psA, gi=gi, fw=fw, fp=fp: v.tensor_scalar(
                    aa[gi][0:fw, :], psA[0:fw, :], ba[0:fw, fp:fp + 1], None,
                    op0=ALU.add), waits=[mm, gev], inc=True, drain=True)
                si, sev = str_.nxt()
                st = stage[si]
                he = k.V(lambda v, st=st, gi=gi, fw=fw: v.tensor_mul(
                    st[0:fw, :], aa[gi][0:fw, :], gg[gi][0:fw, :]),
                    waits=[ge_, ae_, sev, ftile_flush], inc=True, drain=True)
                ggr.done(he)
                psr.done(ae_)
                psgr.done(ge_)
                k.ld(dst[flo:flo + fw, gs], st[0:fw, :], waits=[he], queue="s")
                str_.done(None)
            ftile_flush = k.flush("s")
            wring.done(f_last)
    k.barrier()


def emit_add(k, a_f32, b_f32, out_f32):
    nc = k.nc
    with contextlib.ExitStack() as ph:
        S = lambda n, sh, dt: ph.enter_context(nc.sbuf_tensor(_u(n), sh, dt))
        ta = [S(f"ad_a{i}", [128, T], F32) for i in range(2)]
        tb = [S(f"ad_b{i}", [128, T], F32) for i in range(2)]
        chunk_flush = [None, None]
        for c in range(NCH):
            bi = c % 2
            k.ld(ta[bi][:], a_f32[c * 128:(c + 1) * 128, :], waits=[chunk_flush[bi]], queue="s")
            k.ld(tb[bi][:], b_f32[c * 128:(c + 1) * 128, :], queue="s")
            fl = k.flush("s")
            e = k.V(lambda v, bi=bi: v.tensor_add(ta[bi][:], ta[bi][:], tb[bi][:]),
                    waits=[fl], inc=True, drain=True)
            k.ld(out_f32[c * 128:(c + 1) * 128, :], ta[bi][:], waits=[e], queue="s")
            chunk_flush[bi] = k.flush("s")
    k.barrier()



def build_attn_a(k, IN, xin, xout, scratch):
    xln, qT, kT, vT, ctxT = (scratch["xln"], scratch["qT"], scratch["kT"],
                             scratch["vT"], scratch["ctxT"])
    k.load_consts(["ones_bf", "ident_bf", "ind2_bf"])
    k.barrier()
    emit_ln(k, xin, xln)
    emit_proj(k, xln, IN["L0_attn_a_WqT"], D, dst=qT, bias2d=IN["L0_attn_a_bq2d"], out_scale=0.125)
    emit_proj(k, xln, IN["L0_attn_a_WkT"], D, dst=kT, bias2d=IN["L0_attn_a_bk2d"])
    emit_proj(k, xln, IN["L0_attn_a_WvT"], D, dst=vT, bias2d=IN["L0_attn_a_bv2d"])
    emit_attn(k, qT, kT, vT, ctxT, causal=False, heads_z2d=IN["L0_attn_a_hz2d"])
    emit_proj(k, ctxT, IN["L0_attn_a_WpT"], D, bias2d=IN["L0_attn_a_bp2d"],
              hz2d=IN["L0_attn_a_hidz2d"], resid=xin, resid_out=xout)


def build_l1_attn_a(k, IN, xin, xout, scratch):
    xln, qT, kT, vT, ctxT = (scratch["xln"], scratch["qT"], scratch["kT"],
                             scratch["vT"], scratch["ctxT"])
    k.load_consts(["ones_bf", "ident_bf", "ind2_bf", "rot_il_bf"])
    k.barrier()
    emit_ln(k, xin, xln)
    emit_proj(k, xln, IN["L1_attn_a_wqT"], D, dst=qT, out_scale=0.125)
    emit_proj(k, xln, IN["L1_attn_a_wkT"], D, dst=kT)
    emit_proj(k, xln, IN["L1_attn_a_wvT"], D, dst=vT)
    emit_rope_il(k, qT)
    emit_rope_il(k, kT)
    emit_attn(k, qT, kT, vT, ctxT, causal=True)
    emit_proj(k, ctxT, IN["L1_attn_a_woT"], D, resid=xin, resid_out=xout)


def build_l1_attn_b(k, IN, xin, xout, scratch):
    xln, qT, kT, vT, ctxT = (scratch["xln"], scratch["qT"], scratch["kT"],
                             scratch["vT"], scratch["ctxT"])
    k.load_consts(["ones_bf", "ident_bf", "ind2_bf"])
    k.barrier()
    emit_ln(k, xin, xln)
    caw = IN["L1_attn_b_c_attn_w"]
    emit_proj(k, xln, caw[:, 0:1024], D, dst=qT, bias2d=IN["L1_attn_b_cabq2d"], out_scale=0.125)
    emit_proj(k, xln, caw[:, 1024:2048], D, dst=kT, bias2d=IN["L1_attn_b_cabk2d"])
    emit_proj(k, xln, caw[:, 2048:3072], D, dst=vT, bias2d=IN["L1_attn_b_cabv2d"])
    emit_attn(k, qT, kT, vT, ctxT, causal=True)
    emit_proj(k, ctxT, IN["L1_attn_b_c_proj_w"], D, bias2d=IN["L1_attn_b_cpb2d"],
              resid=xin, resid_out=xout)


def build_attn_b(k, IN, xin, xout, scratch):
    xln, qkvT = scratch["xln"], scratch["qkvT"]
    k.load_consts(["ones_bf", "indrep8_bf"])
    k.barrier()
    emit_ln(k, xin, xln)
    emit_proj(k, xln, IN["L0_attn_b_wqkv"], 3072, dst=qkvT)
    emit_attn_b(k, qkvT, xin, xout)


FFN_STAGES = [99]

def build_l0_ffn(k, IN, xin, xout, scratch):
    xln, qkvT, ctxT = scratch["xln"], scratch["qkvT"], scratch["ctxT"]
    xi, h2T = scratch["xi"], scratch["h2T"]
    k.load_consts(["ones_bf", "ident_bf", "indrep_bf", "rot_hs_bf"])
    k.barrier()
    ns = FFN_STAGES[0]
    emit_ln(k, xin, xln, dst_f32=xi)
    for l in range(2):
        emit_proj(k, xi, IN[f"L0_ffn_wqkv{l}"], 1536, dst=qkvT,
                  shift=True, src_is_f32=True)
        if ns <= l * 10 + 1: break
        emit_l2rope(k, qkvT, 0, IN[f"L0_ffn_qs2d{l}"], "xpos_qc", "xpos_qs")
        emit_l2rope(k, qkvT, 512, IN[f"L0_ffn_ks2d{l}"], "xpos_kc", "xpos_ks")
        if ns <= l * 10 + 2: break
        emit_lattn(k, qkvT, ctxT)
        if ns <= l * 10 + 3: break
        emit_proj(k, ctxT[0:512, :], IN[f"L0_ffn_wo{l}"], D, Din=512,
                  resid=xi, resid_out=xi)
        if ns <= l * 10 + 4: break
        emit_ln(k, xi, xln, gamma_beta=(IN[f"L0_ffn_lg2d{l}"], IN[f"L0_ffn_lb2d{l}"]),
                shift=True)
        emit_glu(k, xln, IN[f"L0_ffn_w1{l}"], IN[f"L0_ffn_b1a2d{l}"],
                 IN[f"L0_ffn_b1g2d{l}"], h2T)
        if ns <= l * 10 + 5: break
        emit_proj(k, h2T, IN[f"L0_ffn_w2{l}"], D, Din=2730, bias2d=IN[f"L0_ffn_b2_2d{l}"],
                  resid=xi, resid_out=xi)
    emit_add(k, xin, xi, xout)


def build_moe(k, IN, xin, xout, scratch):
    xln = scratch["xln"]
    k.load_consts(["ones_bf", "ident_f32"])
    k.barrier()
    emit_ln(k, xin, xln)
    emit_moe(k, xln, IN["L1_ffn_rwT"], IN["L1_ffn_expert_weights"], xin, xout)



def _prep_weights(inputs):
    P = {}
    f32 = np.float32
    def t(x):
        return np.ascontiguousarray(np.asarray(x, f32).T)
    def vec2d(x):
        return np.ascontiguousarray(np.asarray(x, f32).reshape(-1, 128).T)
    P["L0_attn_a_WqT"] = t(inputs["L0_attn_a_Wq"])
    P["L0_attn_a_WkT"] = t(inputs["L0_attn_a_Wk"])
    P["L0_attn_a_WvT"] = t(inputs["L0_attn_a_Wv"])
    P["L0_attn_a_WpT"] = t(inputs["L0_attn_a_Wp"])
    P["L0_attn_a_bq2d"] = vec2d(inputs["L0_attn_a_bq"])
    P["L0_attn_a_bk2d"] = vec2d(inputs["L0_attn_a_bk"])
    P["L0_attn_a_bv2d"] = vec2d(inputs["L0_attn_a_bv"])
    P["L0_attn_a_bp2d"] = vec2d(inputs["L0_attn_a_bp"])
    P["L0_attn_a_hz2d"] = np.ascontiguousarray(
        np.asarray(inputs["L0_attn_a_heads_z"], f32).reshape(1, 16))
    P["L0_attn_a_hidz2d"] = vec2d(inputs["L0_attn_a_hidden_z"])
    P["L1_attn_a_wqT"] = t(inputs["L1_attn_a_wq"])
    P["L1_attn_a_wkT"] = t(inputs["L1_attn_a_wk"])
    P["L1_attn_a_wvT"] = t(inputs["L1_attn_a_wv"])
    P["L1_attn_a_woT"] = t(inputs["L1_attn_a_wo"])
    P["L1_attn_b_c_attn_w"] = np.ascontiguousarray(np.asarray(inputs["L1_attn_b_c_attn_w"], f32))
    P["L1_attn_b_c_proj_w"] = np.ascontiguousarray(np.asarray(inputs["L1_attn_b_c_proj_w"], f32))
    cab = np.asarray(inputs["L1_attn_b_c_attn_b"], f32)
    P["L1_attn_b_cabq2d"] = vec2d(cab[0:1024])
    P["L1_attn_b_cabk2d"] = vec2d(cab[1024:2048])
    P["L1_attn_b_cabv2d"] = vec2d(cab[2048:3072])
    P["L1_attn_b_cpb2d"] = vec2d(inputs["L1_attn_b_c_proj_b"])
    P["L1_ffn_rwT"] = t(inputs["L1_ffn_router_w"])
    P["L1_ffn_expert_weights"] = np.ascontiguousarray(np.asarray(inputs["L1_ffn_expert_weights"], f32))
    wqkv = np.asarray(inputs["L0_attn_b_wqkv"], f32)
    perm = np.concatenate([np.arange(16) * 192 + s * 64 + np.arange(64)[:, None]
                           for s in range(3)], axis=1).T.reshape(-1)
    perm = np.zeros(3072, np.int64)
    for s in range(3):
        for h in range(16):
            for d_ in range(64):
                perm[s * 1024 + h * 64 + d_] = 192 * h + 64 * s + d_
    P["L0_attn_b_wqkv"] = np.ascontiguousarray(wqkv[:, perm])
    HID = 2730
    for l in range(2):
        P[f"L0_ffn_wqkv{l}"] = np.ascontiguousarray(np.asarray(inputs["L0_ffn_wqkv"], f32)[l])
        P[f"L0_ffn_wo{l}"] = np.ascontiguousarray(np.asarray(inputs["L0_ffn_wo"], f32)[l])
        P[f"L0_ffn_w1{l}"] = np.ascontiguousarray(np.asarray(inputs["L0_ffn_w1"], f32)[l])
        P[f"L0_ffn_w2{l}"] = np.ascontiguousarray(np.asarray(inputs["L0_ffn_w2"], f32)[l])
        qs = np.asarray(inputs["L0_ffn_q_scale"], f32)[l]
        ks = np.asarray(inputs["L0_ffn_k_scale"], f32)[l]
        P[f"L0_ffn_qs2d{l}"] = np.ascontiguousarray(np.concatenate([qs, qs]).reshape(128, 1))
        P[f"L0_ffn_ks2d{l}"] = np.ascontiguousarray(np.concatenate([ks, ks]).reshape(128, 1))
        P[f"L0_ffn_lg2d{l}"] = vec2d(np.asarray(inputs["L0_ffn_ln_g"], f32)[l])
        P[f"L0_ffn_lb2d{l}"] = vec2d(np.asarray(inputs["L0_ffn_ln_b"], f32)[l])
        b1 = np.asarray(inputs["L0_ffn_b1"], f32)[l]
        npair = 22
        b1a = np.zeros((128, npair), f32)
        b1g = np.zeros((128, npair), f32)
        for j in range(npair):
            n = min(HID - j * 128, 128)
            b1a[0:n, j] = b1[j * 128:j * 128 + n]
            b1g[0:n, j] = b1[HID + j * 128:HID + j * 128 + n]
        P[f"L0_ffn_b1a2d{l}"] = b1a
        P[f"L0_ffn_b1g2d{l}"] = b1g
        P[f"L0_ffn_b2_2d{l}"] = vec2d(np.asarray(inputs["L0_ffn_b2"], f32)[l])
    return P

FULL_ORDER = ["attn_a", "l0_ffn", "attn_b", "l0_ffn", "l1_attn_a", "moe", "l1_attn_b", "moe"]
SUBLAYERS = list(FULL_ORDER)
SUB_W = {
    "attn_a": ["L0_attn_a_WqT", "L0_attn_a_WkT", "L0_attn_a_WvT", "L0_attn_a_WpT",
               "L0_attn_a_bq2d", "L0_attn_a_bk2d", "L0_attn_a_bv2d", "L0_attn_a_bp2d",
               "L0_attn_a_hz2d", "L0_attn_a_hidz2d"],
    "l1_attn_a": ["L1_attn_a_wqT", "L1_attn_a_wkT", "L1_attn_a_wvT", "L1_attn_a_woT"],
    "l1_attn_b": ["L1_attn_b_c_attn_w", "L1_attn_b_c_proj_w", "L1_attn_b_cabq2d",
                  "L1_attn_b_cabk2d", "L1_attn_b_cabv2d", "L1_attn_b_cpb2d"],
    "moe": ["L1_ffn_rwT", "L1_ffn_expert_weights"],
    "attn_b": ["L0_attn_b_wqkv"],
    "l0_ffn": (["L0_ffn_wqkv0", "L0_ffn_wo0", "L0_ffn_w10", "L0_ffn_w20",
                "L0_ffn_qs2d0", "L0_ffn_ks2d0", "L0_ffn_lg2d0", "L0_ffn_lb2d0",
                "L0_ffn_b1a2d0", "L0_ffn_b1g2d0", "L0_ffn_b2_2d0"] +
               ["L0_ffn_wqkv1", "L0_ffn_wo1", "L0_ffn_w11", "L0_ffn_w21",
                "L0_ffn_qs2d1", "L0_ffn_ks2d1", "L0_ffn_lg2d1", "L0_ffn_lb2d1",
                "L0_ffn_b1a2d1", "L0_ffn_b1g2d1", "L0_ffn_b2_2d1"]),
}

def _build_nc(sub, wshapes):
    nc = bass.Bass("TRN2", target_bir_lowering=False, debug=False)
    es = contextlib.ExitStack()
    nc._es = es
    k = Ktx(nc, es)
    IN = {}
    for name, shp in wshapes.items():
        IN[name] = nc.dram_tensor(name, list(shp), F32, kind="ExternalInput").ap()
    xin = nc.dram_tensor("xT", [D, T], F32, kind="ExternalInput").ap()
    xout = nc.dram_tensor("xT_out", [D, T], F32, kind="ExternalOutput").ap()
    if sub == "moe" and _os.environ.get("MOE_DBG"):
        MOE_DBG[0] = nc.dram_tensor("moe_dbg", [128, 4096], F32, kind="ExternalOutput").ap()
    scratch = {
        "xln": nc.dram_tensor("s_xln", [D, T], BF16).ap(),
        "qT": nc.dram_tensor("s_qT", [D, T], BF16).ap(),
        "kT": nc.dram_tensor("s_kT", [D, T], BF16).ap(),
        "vT": nc.dram_tensor("s_vT", [D, T], BF16).ap(),
        "ctxT": nc.dram_tensor("s_ctxT", [D, T], BF16).ap(),
        "qkvT": nc.dram_tensor("s_qkvT", [3072, T], BF16).ap(),
        "xi": nc.dram_tensor("s_xi", [D, T], F32).ap(),
        "h2T": nc.dram_tensor("s_h2T", [2730, T], BF16).ap(),
    }
    k.drows = nc.dram_tensor("s_rows", [16, T], F32).ap()
    es.__enter__()
    builders = {"attn_a": build_attn_a, "l1_attn_a": build_l1_attn_a,
                "l1_attn_b": build_l1_attn_b, "moe": build_moe,
                "attn_b": build_attn_b, "l0_ffn": build_l0_ffn}
    builders[sub](k, IN, xin, xout, scratch)
    k.barrier()
    k.finalize()
    es.__exit__(None, None, None)
    return nc, list(k.dram_consts.keys())


def run_sublayer(sub, P, xTs):
    wnames = {n: P[n].shape for n in SUB_W[sub]}
    nc, cnames = _build_nc(sub, wnames)
    base = {n: P[n] for n in wnames}
    for cn in cnames:
        base["const_" + cn] = np.ascontiguousarray(CONSTS[cn])
    in_maps = [dict(base, xT=np.ascontiguousarray(x)) for x in xTs]
    res = run_bass_kernel_spmd(nc, in_maps, core_ids=list(range(8)))
    if MOE_DBG[0] is not None:
        np.save("_moe_dbg.npy", np.asarray(res.results[0]["moe_dbg"], np.float32))
    return [np.asarray(r["xT_out"], np.float32) for r in res.results]


def _build_full_nc(wshapes):
    nc = bass.Bass("TRN2", target_bir_lowering=False, debug=False)
    es = contextlib.ExitStack()
    nc._es = es
    k = Ktx(nc, es)
    IN = {}
    for name, shp in wshapes.items():
        IN[name] = nc.dram_tensor(name, list(shp), F32, kind="ExternalInput").ap()
    xext = nc.dram_tensor("xT", [D, T], F32, kind="ExternalInput").ap()
    xout_ext = nc.dram_tensor("xT_out", [D, T], F32, kind="ExternalOutput").ap()
    xA = nc.dram_tensor("s_xA", [D, T], F32).ap()
    xB = nc.dram_tensor("s_xB", [D, T], F32).ap()
    scratch = {
        "xln": nc.dram_tensor("s_xln", [D, T], BF16).ap(),
        "qT": nc.dram_tensor("s_qT", [D, T], BF16).ap(),
        "kT": nc.dram_tensor("s_kT", [D, T], BF16).ap(),
        "vT": nc.dram_tensor("s_vT", [D, T], BF16).ap(),
        "ctxT": nc.dram_tensor("s_ctxT", [D, T], BF16).ap(),
        "qkvT": nc.dram_tensor("s_qkvT", [3072, T], BF16).ap(),
        "xi": nc.dram_tensor("s_xi", [D, T], F32).ap(),
        "h2T": nc.dram_tensor("s_h2T", [2730, T], BF16).ap(),
    }
    k.drows = nc.dram_tensor("s_rows", [16, T], F32).ap()
    es.__enter__()
    builders = {"attn_a": build_attn_a, "l1_attn_a": build_l1_attn_a,
                "l1_attn_b": build_l1_attn_b, "moe": build_moe,
                "attn_b": build_attn_b, "l0_ffn": build_l0_ffn}
    cur = xext
    nxt = xA
    for i, sub in enumerate(FULL_ORDER):
        out = xout_ext if i == len(FULL_ORDER) - 1 else nxt
        builders[sub](k, IN, cur, out, scratch)
        cur, nxt = out, (xB if out is xA else xA)
    k.barrier()
    k.finalize()
    es.__exit__(None, None, None)
    return nc, list(k.dram_consts.keys())


def kernel(**inputs):
    P = _prep_weights(inputs)
    x = np.asarray(inputs["L0_attn_a_inputs"], np.float32)
    xTs = [np.ascontiguousarray(x[b].T) for b in range(8)]
    if _os.environ.get("KMULTI"):
        for sub in SUBLAYERS:
            xTs = run_sublayer(sub, P, xTs)
    else:
        wnames = {}
        for sub in set(FULL_ORDER):
            for n in SUB_W[sub]:
                wnames[n] = P[n].shape
        nc, cnames = _build_full_nc(wnames)
        base = {n: P[n] for n in wnames}
        for cn in cnames:
            base["const_" + cn] = np.ascontiguousarray(CONSTS[cn])
        in_maps = [dict(base, xT=np.ascontiguousarray(xt)) for xt in xTs]
        res = run_bass_kernel_spmd(nc, in_maps, core_ids=list(range(8)))
        xTs = [np.asarray(r["xT_out"], np.float32) for r in res.results]
    out = np.stack([xt.T for xt in xTs], 0)
    return out.astype(np.float32)
